# version 01
# baseline (speedup 1.0000x reference)
import numpy as np
import ml_dtypes
import concourse.bass as bass
import concourse.mybir as mybir
from concourse import library_config
from concourse.bass_utils import run_bass_kernel_spmd
from contextlib import ExitStack

F32 = mybir.dt.float32
BF16 = mybir.dt.bfloat16
I16 = mybir.dt.int16
U32 = mybir.dt.uint32
AF = mybir.ActivationFunctionType
ALU = mybir.AluOpType
AXL = mybir.AxisListType

L = 4096
D = 1024
NCH_IP = 97
CH = 128
TG = 512
NCHUNK = TG // CH
NTILE = L // 128
NEG = -30000.0
EPS = 1e-5

DEBUG = set()
N_GROUPS = L // TG
DO_MAMBA = True
DO_FFN_A = True
DO_FFN_B = True
DO_FFN_C = True
N_FFN_TILES = NTILE
FA_CUT = 99


class Sem:
    __slots__ = ("h", "n", "name")

    def __init__(self, h, name):
        self.h = h
        self.n = 0
        self.name = name


class Prog:
    ENGINES = ("sync", "vector", "scalar", "tensor", "gpsimd")

    def __init__(self, nc, stack):
        self.nc = nc
        self.stack = stack
        self.prog = {e: [] for e in self.ENGINES}
        self.n_inst = 0

    def sem(self, name):
        return Sem(self.stack.enter_context(self.nc.semaphore(name)), name)

    def step(self, eng, fn, wait=(), inc=()):
        wv = [(s, s.n if v is None else v) for (s, v) in wait]
        for s, k in inc:
            s.n += k
        self.n_inst += 1

        def run(engine, wv=wv, inc=tuple(inc), fn=fn):
            for s, v in wv:
                if v > 0:
                    engine.wait_ge(s.h, v)
            inst = fn(engine)
            for s, k in inc:
                inst = inst.then_inc(s.h, k)

        self.prog[eng].append(run)

    def replay(self):
        nc = self.nc
        with nc.Block() as block:
            @block.sync
            def _(eng):
                for f in self.prog["sync"]:
                    f(eng)

            @block.vector
            def _(eng):
                for f in self.prog["vector"]:
                    f(eng)

            @block.scalar
            def _(eng):
                for f in self.prog["scalar"]:
                    f(eng)

            @block.tensor
            def _(eng):
                for f in self.prog["tensor"]:
                    f(eng)

            @block.gpsimd
            def _(eng):
                for f in self.prog["gpsimd"]:
                    f(eng)


def build_nc():
    nc = bass.Bass("TRN2", target_bir_lowering=False)
    stack = ExitStack()
    p = Prog(nc, stack)

    def din(name, shape, dtype=F32):
        return nc.dram_tensor(name, list(shape), dtype, kind="ExternalInput").ap()

    x_d = din("x", (L, D))
    ipj_d = din("w_ipj", (NCH_IP, D, 128))
    convw_d = din("w_convw", (128, 80, 4))
    convb_d = din("w_convb", (128, 80))
    dtb_d = din("w_dtb", (32, 1))
    alog_d = din("w_alog", (32, 1))
    dskip_d = din("w_dskip", (128, 16))
    gnw_d = din("w_gnw", (128, 16))
    normw_d = din("w_normw", (128, 8))
    opj_d = din("w_opj", (16, 128, D))
    wq_d = din("w_wq", (128, 8, 512))
    keys_d = din("w_keys", (128, 512))
    wup_d = din("w_wup", (128, 8, 4096))
    wdn_d = din("w_wdn", (128, 32, D))
    det_d = din("w_det", (128, 8, 4096))
    ue_d = din("w_ue", (128, 32, D))
    ident_d = din("c_ident", (128, 128), BF16)
    tm_d = din("c_tm", (128, 128))
    ones_d = din("c_ones", (128, 512))
    misc_d = din("c_misc", (128, 4))
    identf_d = din("c_identf", (128, 128))
    iota8_d = din("c_iota8", (128, 16))
    onesb_d = din("c_onesb", (128, 1), BF16)

    out_d = nc.dram_tensor("out", [L, D], F32, kind="ExternalOutput").ap()

    dbg = {}

    def dout(name, shape, dtype=F32, force=False):
        if (name in DEBUG or force) and name not in dbg:
            dbg[name] = nc.dram_tensor("dbg_" + name, list(shape), dtype,
                                       kind="ExternalOutput").ap()
        return dbg.get(name)

    d_ipbf = nc.dram_tensor("d_ipbf", [NCH_IP, 128, 8, 128], BF16, kind="Internal").ap()
    d_x1 = nc.dram_tensor("d_x1", [L, D], F32, kind="Internal").ap()
    d_cum = nc.dram_tensor("d_cum", [32, TG], F32, kind="Internal").ap()
    d_hid = nc.dram_tensor("d_hid", [NTILE, 8, 128, 128], BF16, kind="Internal").ap()
    d_up = nc.dram_tensor("d_up", [32, 128, L], BF16, kind="Internal").ap()
    d_mlp = nc.dram_tensor("d_mlp", [L, D], BF16, kind="Internal").ap()
    d_pc = nc.dram_tensor("d_pc", [NTILE, 128, 32], BF16, kind="Internal").ap()
    d_ix = nc.dram_tensor("d_ix", [NTILE, 128, 96], I16, kind="Internal").ap()

    s_wt = p.sem("wt")
    s_ax = p.sem("ax")
    s_v = p.sem("v")
    s_s = p.sem("s")
    s_t = p.sem("t")
    s_g = p.sem("g")

    WMAP = {"sync": s_wt, "gpsimd2": s_ax, "vector": s_v, "scalar": s_s,
            "tensor": s_t, "gpsimd": s_g}

    def aw(*engs):
        return [(WMAP[e], None) for e in engs]

    def sb(name, shape, dtype):
        return stack.enter_context(nc.sbuf_tensor("sb_" + name, list(shape), dtype))

    def pt(name, shape, dtype=F32):
        return stack.enter_context(nc.psum_tensor(name, list(shape), dtype))

    def arena():
        st = ExitStack()

        def sb2(name, shape, dtype):
            return st.enter_context(nc.sbuf_tensor("sb_" + name, list(shape), dtype))
        return st, sb2

    def dma(eng, out, in_, wait=()):
        sem = s_wt if eng == "sync" else s_ax
        p.step(eng, lambda e, o=out, i=in_: e.dma_start(out=o, in_=i),
               wait=wait, inc=[(sem, 16)])

    c_id = sb("c_id", (128, 128), BF16)
    c_tm = sb("c_tm", (128, 128), F32)
    c_ones = sb("c_ones", (128, 512), F32)
    c_onesb = sb("c_onesb", (128, 1), BF16)
    c_misc = sb("c_misc", (128, 4), F32)
    c_idf = sb("c_idf", (128, 128), F32)
    c_iota8 = sb("c_iota8", (128, 16), F32)
    w_convw = sb("w_convw", (128, 80, 4), F32)
    w_convb = sb("w_convb", (128, 80), F32)
    w_dtb = sb("w_dtb", (32, 1), F32)
    w_acol = sb("w_acol", (32, 1), F32)
    w_dskip = sb("w_dskip", (128, 16), F32)
    w_gnw = sb("w_gnw", (128, 16), F32)
    w_normw = sb("w_normw", (128, 8), F32)

    for dst, src in ((c_id[:], ident_d), (c_tm[:], tm_d), (c_ones[:], ones_d),
                     (c_onesb[:], onesb_d), (c_misc[:], misc_d), (c_idf[:], identf_d), (c_iota8[:], iota8_d),
                     (w_convw[:], convw_d), (w_convb[:], convb_d),
                     (w_dtb[:], dtb_d), (w_acol[:], alog_d), (w_dskip[:], dskip_d),
                     (w_gnw[:], gnw_d), (w_normw[:], normw_d)):
        dma("sync", dst, src)

    p.step("scalar", lambda e: e.activation(out=w_acol[:], in_=w_acol[:], func=AF.Exp),
           wait=aw("sync"), inc=[(s_s, 1)])
    p.step("vector", lambda e: e.tensor_scalar(out=w_acol[:], in0=w_acol[:], scalar1=-1.0,
                                               scalar2=None, op0=ALU.mult),
           wait=aw("scalar"), inc=[(s_v, 1)])
    p.step("gpsimd", lambda e: e.load_library(library_config.local_scatter))

    pOP = pt("pOP", (128, 1024))
    pmA = pt("pmA", (128, 512))
    ptr = pt("ptr", (128, 1024), BF16)
    psc = pt("psc", (128, 2048))

    x_t = sb("x_t", (128, 1024), F32)
    tb_t = sb("tb_t", (128, 1024), BF16)
    u_t = sb("u_t", (128, 1024), F32)
    ao_t = sb("ao_t", (128, 1024), F32)
    mcol = sb("mcol", (128, 16), F32)
    vcol = sb("vcol", (128, 16), F32)
    rsq1 = sb("rsq1", (128, 4), F32)
    rsq2 = sb("rsq2", (128, 4), F32)
    tmpc = sb("tmpc", (128, 8), F32)

    def ln_stats(src_ap, mc, vc):
        p.step("vector", lambda e: e.tensor_reduce(
            out=mc, in_=src_ap, axis=AXL.X, op=ALU.add),
            wait=aw("sync", "scalar", "vector"), inc=[(s_v, 1)])
        p.step("scalar", lambda e: e.activation(
            out=u_t[:], in_=src_ap, func=AF.Square),
            wait=aw("sync", "scalar", "vector"), inc=[(s_s, 1)])
        p.step("vector", lambda e: e.tensor_reduce(
            out=vc, in_=u_t[:], axis=AXL.X, op=ALU.add),
            wait=aw("scalar", "vector"), inc=[(s_v, 1)])
        p.step("vector", lambda e: e.tensor_scalar(
            out=mc, in0=mc, scalar1=1.0 / 1024, scalar2=None, op0=ALU.mult),
            wait=aw("vector"), inc=[(s_v, 1)])
        p.step("vector", lambda e: e.tensor_tensor(
            out=tmpc[:, 0:1], in0=mc, in1=mc, op=ALU.mult),
            wait=aw("vector"), inc=[(s_v, 1)])
        p.step("vector", lambda e: e.tensor_scalar(
            out=vc, in0=vc, scalar1=1.0 / 1024, scalar2=tmpc[:, 0:1],
            op0=ALU.mult, op1=ALU.subtract), wait=aw("vector"), inc=[(s_v, 1)])

    def rsq_from_var(vc, dst, scale_in=1.0, bias_col=0):
        p.step("scalar", lambda e: e.activation(
            out=dst, in_=vc, func=AF.Ln, bias=c_misc[:, bias_col:bias_col + 1],
            scale=scale_in), wait=aw("vector", "scalar"), inc=[(s_s, 1)])
        p.step("scalar", lambda e: e.activation(
            out=dst, in_=dst, func=AF.Exp, scale=-0.5),
            wait=aw("scalar"), inc=[(s_s, 1)])

    if DO_MAMBA:
        m_st, sb = arena()
        ipslab = [sb("ipslab0", (128, 8, 128), BF16), sb("ipslab1", (128, 8, 128), BF16)]
        ru_fm = sb("ru_fm", (128, 8, 512), BF16)
        xp0 = sb("xp0", (128, 516), BF16)
        xp = [xp0, xp0]
        cacc = sb("cacc", (128, 512), BF16)
        tails = sb("tails", (128, 80, 4), BF16)
        z_fm = sb("z_fm", (128, 16, 512), BF16)
        X_fm = sb("X_fm", (128, 16, 512), BF16)
        B_fm = sb("B_fm", (128, 32, 512), BF16)
        C_fm = sb("C_fm", (128, 32, 512), BF16)
        dt_s = sb("dt_s", (32, 512), F32)
        sps = sb("sps", (32, 512), F32)
        ln_s = sb("ln_s", (32, 512), F32)
        am_s = sb("am_s", (32, 512), F32)
        ct_fm = sb("ct_fm", (32, 128), F32)
        wf_fm = sb("wf_fm", (32, 128), F32)
        ct_tok = sb("ct_tok", (128, 32), F32)
        wf_tok = sb("wf_tok", (128, 32), F32)
        cumrow = sb("cumrow", (128, 2048), F32)
        decay = sb("decay", (128, 2048), BF16)
        expcr = sb("expcr", (128, 2048), BF16)
        scoresT = sb("scoresT", (128, 16, 128), BF16)
        B_tok = sb("B_tok", (128, 32, 128), BF16)
        X_tok = sb("X_tok", (128, 16, 128), BF16)
        Ctil = sb("Ctil", (128, 16, 128), BF16)
        Hbf = sb("Hbf", (128, 16, 64), BF16)
        carry = sb("carry", (128, 32, 64), F32)
        ydsk = sb("ydsk", (128, 8, 128), BF16)
        y_g = sb("y_g", (128, 8, 128), F32)
        ysq = ydsk
        y2_fm = sb("y2_fm", (128, 16, 512), BF16)
        rstd_c = sb("rstd_c", (128, 4), F32)

        p.step("vector", lambda e: e.memset(carry[:].rearrange("p a b -> p (a b)"), 0.0),
               inc=[(s_v, 1)])
        p.step("vector", lambda e: e.memset(tails[:].rearrange("p a b -> p (a b)"), 0.0),
               inc=[(s_v, 1)])

        for g in range(N_GROUPS):
            t0 = g * TG
            for tc in range(NCHUNK):
                row = t0 + tc * CH
                dma("sync", x_t[:], x_d[row:row + 128, :],
                    wait=aw("vector", "scalar", "tensor"))
                ln_stats(x_t[:], mcol[:, tc:tc + 1], vcol[:, tc:tc + 1])
                rsq_from_var(vcol[:, tc:tc + 1], rsq1[:, tc:tc + 1])
                rsq_from_var(vcol[:, tc:tc + 1], rsq2[:, tc:tc + 1],
                             scale_in=1.0 + EPS, bias_col=1)
                p.step("vector", lambda e, tc=tc: e.scalar_tensor_tensor(
                    out=tmpc[:, 1:2], in0=mcol[:, tc:tc + 1], scalar=-1.0,
                    in1=rsq2[:, tc:tc + 1], op0=ALU.mult, op1=ALU.mult),
                    wait=aw("scalar", "vector"), inc=[(s_v, 1)])
                p.step("scalar", lambda e, tc=tc: e.activation(
                    out=tb_t[:], in_=x_t[:], func=AF.Identity, bias=tmpc[:, 1:2],
                    scale=rsq2[:, tc:tc + 1]), wait=aw("vector", "tensor"), inc=[(s_s, 1)])
                if g == 0 and tc == 0 and "ln1" in DEBUG:
                    for nm, t in (("ln_x", x_t[:]), ("ln_m", mcol[:]),
                                  ("ln_v", vcol[:]), ("ln_r1", rsq1[:]),
                                  ("ln_r2", rsq2[:])):
                        d = dout(nm, tuple(t.shape), F32, force=True)
                        dma("sync", d, t, wait=aw("scalar", "vector"))
                    d = dout("ln_tb", (128, 1024), BF16, force=True)
                    dma("sync", d, tb_t[:], wait=aw("scalar", "vector"))
                for kt in range(8):
                    p.step("tensor", lambda e, kt=kt: e.transpose(
                        ptr[:, kt * 128:(kt + 1) * 128],
                        tb_t[:, kt * 128:(kt + 1) * 128], c_id[:]),
                        wait=aw("scalar", "vector") if kt == 0 else (),
                        inc=[(s_t, 1)] if kt == 7 else ())
                p.step("vector", lambda e, tc=tc: e.tensor_tensor(
                    out=ru_fm[:, :, tc * 128:(tc + 1) * 128],
                    in0=ptr[:].rearrange("p (k t) -> p k t", k=8),
                    in1=w_normw[:].rearrange("p k -> p k ()").to_broadcast([128, 8, 128]),
                    op=ALU.mult), wait=aw("tensor"), inc=[(s_v, 1)])

            if g == 0 and dout("ru_fm", (128, 8, 512), BF16) is not None:
                dma("sync", dbg["ru_fm"], ru_fm[:], wait=aw("vector"))

            for c in range(NCH_IP):
                buf = ipslab[c % 2]
                src = (ipj_d[c].rearrange("(k d) m -> d k m", k=8)
                       if g == 0 else d_ipbf[c])
                p.step("gpsimd", lambda e, buf=buf, src=src: e.dma_start(
                    out=buf[:], in_=src), wait=aw("tensor", "sync"), inc=[(s_ax, 16)])
                if g == 0:
                    dma("sync", d_ipbf[c], buf[:], wait=aw("gpsimd2"))
                pmm = pOP[:, 0:512] if c % 2 == 0 else pOP[:, 512:1024]
                for kt in range(8):
                    p.step("tensor", lambda e, buf=buf, kt=kt, pmm=pmm: e.matmul(
                        pmm, buf[:, kt, :], ru_fm[:, kt, :],
                        start=(kt == 0), stop=(kt == 7)),
                        wait=aw("gpsimd2", "vector", "scalar") if kt == 0 else (),
                        inc=[(s_t, 1)] if kt == 7 else ())
                if c < 16:
                    p.step("scalar", lambda e, c=c, pmm=pmm: e.activation(
                        out=z_fm[:, c, :], in_=pmm, func=AF.Silu),
                        wait=aw("tensor"), inc=[(s_s, 1)])
                elif c < 96:
                    cc = c - 16
                    xbuf = xp[c % 2]
                    p.step("vector", lambda e, cc=cc, xbuf=xbuf: e.tensor_copy(
                        out=xbuf[:, 0:3], in_=tails[:, cc, 0:3]),
                        wait=aw("tensor", "scalar"), inc=[(s_v, 1)])
                    p.step("scalar", lambda e, xbuf=xbuf, pmm=pmm: e.activation(
                        out=xbuf[:, 3:515], in_=pmm, func=AF.Copy),
                        wait=aw("vector", "tensor"), inc=[(s_s, 1)])
                    p.step("vector", lambda e, cc=cc, xbuf=xbuf: e.tensor_copy(
                        out=tails[:, cc, 0:3], in_=xbuf[:, 512:515]),
                        wait=aw("scalar"), inc=[(s_v, 1)])
                    p.step("vector", lambda e, cc=cc, xbuf=xbuf: e.tensor_scalar(
                        out=cacc[:], in0=xbuf[:, 0:512], scalar1=w_convw[:, cc, 0:1],
                        scalar2=None, op0=ALU.mult), wait=aw("scalar"), inc=[(s_v, 1)])
                    for k in range(1, 4):
                        p.step("vector", lambda e, cc=cc, k=k, xbuf=xbuf: e.scalar_tensor_tensor(
                            out=cacc[:], in0=xbuf[:, k:k + 512],
                            scalar=w_convw[:, cc, k:k + 1],
                            in1=cacc[:], op0=ALU.mult, op1=ALU.add), inc=[(s_v, 1)])
                    if cc < 16:
                        dst = X_fm[:, cc, :]
                    elif cc < 48:
                        dst = B_fm[:, cc - 16, :]
                    else:
                        dst = C_fm[:, cc - 48, :]
                    p.step("scalar", lambda e, cc=cc, dst=dst: e.activation(
                        out=dst, in_=cacc[:], func=AF.Silu, bias=w_convb[:, cc:cc + 1]),
                        wait=aw("vector"), inc=[(s_s, 1)])
                else:
                    p.step("scalar", lambda e, pmm=pmm: e.activation(
                        out=sps[:, :], in_=pmm[0:32, :], func=AF.Abs, bias=w_dtb[:]),
                        wait=aw("tensor"), inc=[(s_s, 1)])
                    p.step("scalar", lambda e, pmm=pmm: e.activation(
                        out=dt_s[:, :], in_=pmm[0:32, :], func=AF.Relu, bias=w_dtb[:]),
                        wait=aw("scalar"), inc=[(s_s, 1)])
                    p.step("scalar", lambda e: e.activation(
                        out=sps[:, :], in_=sps[:, :], func=AF.Exp, scale=-1.0),
                        wait=aw("scalar"), inc=[(s_s, 1)])
                    p.step("scalar", lambda e: e.activation(
                        out=sps[:, :], in_=sps[:, :], func=AF.Ln, bias=1.0),
                        wait=aw("scalar"), inc=[(s_s, 1)])
                    p.step("vector", lambda e: e.tensor_tensor(
                        out=dt_s[:, :], in0=dt_s[:, :], in1=sps[:, :], op=ALU.add),
                        wait=aw("scalar"), inc=[(s_v, 1)])
                    p.step("scalar", lambda e: e.activation(
                        out=ln_s[:, :], in_=dt_s[:, :], func=AF.Ln),
                        wait=aw("vector"), inc=[(s_s, 1)])
                    p.step("vector", lambda e: e.tensor_scalar(
                        out=am_s[:, :], in0=dt_s[:, :], scalar1=w_acol[:],
                        scalar2=None, op0=ALU.mult), wait=aw("scalar"), inc=[(s_v, 1)])
                    for c2 in range(NCHUNK):
                        p.step("vector", lambda e, c2=c2: e.tensor_tensor_scan(
                            out=sps[:, c2 * 128:(c2 + 1) * 128],
                            data0=c_ones[0:32, 0:128],
                            data1=am_s[:, c2 * 128:(c2 + 1) * 128],
                            initial=0.0, op0=ALU.mult, op1=ALU.add),
                            wait=aw("vector"), inc=[(s_v, 1)])
                    dma("sync", d_cum[:, :], sps[:, :], wait=aw("vector"))

            if g == 0:
                for nm, t, sh in (("z_fm", z_fm, (128, 16, 512)),
                                  ("X_fm", X_fm, (128, 16, 512)),
                                  ("B_fm", B_fm, (128, 32, 512)),
                                  ("C_fm", C_fm, (128, 32, 512))):
                    if dout(nm, sh, BF16) is not None:
                        dma("sync", dbg[nm], t[:], wait=aw("scalar", "vector"))
                for nm, t in (("dt_fm", dt_s[:, :]), ("cum_fm", sps[:, :]),
                              ("lndt", ln_s[:, :])):
                    if dout(nm, (32, 512)) is not None:
                        dma("sync", dbg[nm], t, wait=aw("vector", "scalar", "sync"))

            for cc in range(NCHUNK):
                cs = slice(cc * 128, (cc + 1) * 128)
                p.step("vector", lambda e, cs=cs: e.tensor_tensor(
                    out=ct_fm[:, :], in0=ln_s[:, cs], in1=sps[:, cs],
                    op=ALU.subtract), wait=aw("scalar", "sync"), inc=[(s_v, 1)])
                p.step("vector", lambda e, cs=cs, cc=cc: e.tensor_scalar(
                    out=wf_fm[:, :], in0=sps[:, cs], scalar1=-1.0,
                    scalar2=sps[:, cc * 128 + 127:cc * 128 + 128],
                    op0=ALU.mult, op1=ALU.add), wait=aw("vector"), inc=[(s_v, 1)])
                p.step("vector", lambda e, cs=cs: e.tensor_tensor(
                    out=wf_fm[:, :], in0=wf_fm[:, :], in1=ln_s[:, cs], op=ALU.add),
                    wait=aw("vector", "scalar"), inc=[(s_v, 1)])
                p.step("scalar", lambda e: e.activation(
                    out=wf_fm[:, :], in_=wf_fm[:, :], func=AF.Exp),
                    wait=aw("vector"), inc=[(s_s, 1)])
                p.step("tensor", lambda e: e.transpose(
                    pmA[:, 0:32], ct_fm[:, :], c_idf[0:32, 0:32]),
                    wait=aw("vector", "scalar"), inc=[(s_t, 1)])
                p.step("tensor", lambda e: e.transpose(
                    pmA[:, 32:64], wf_fm[:, :], c_idf[0:32, 0:32]), inc=[(s_t, 1)])
                p.step("vector", lambda e: e.tensor_copy(
                    out=ct_tok[:], in_=pmA[:, 0:32]), wait=aw("tensor"), inc=[(s_v, 1)])
                p.step("vector", lambda e: e.tensor_copy(
                    out=wf_tok[:], in_=pmA[:, 32:64]), wait=aw("vector"), inc=[(s_v, 1)])

                for blk in range(6):
                    def _tok_tr(e, blk=blk, cs=cs):
                        for j in range(8):
                            if blk < 2:
                                srcp = X_fm[:, blk * 8 + j, cs]
                            else:
                                srcp = B_fm[:, (blk - 2) * 8 + j, cs]
                            inst = e.transpose(ptr[:, j * 128:(j + 1) * 128],
                                               srcp, c_id[:])
                        return inst
                    p.step("tensor", _tok_tr, wait=aw("vector", "scalar"),
                           inc=[(s_t, 1)])
                    if blk < 2:
                        p.step("vector", lambda e, blk=blk: e.tensor_copy(
                            out=X_tok[:, blk * 8:(blk + 1) * 8, :].rearrange("p a b -> p (a b)"),
                            in_=ptr[:]), wait=aw("tensor"), inc=[(s_v, 1)])
                    else:
                        p.step("vector", lambda e, blk=blk: e.tensor_tensor(
                            out=B_tok[:, (blk - 2) * 8:(blk - 1) * 8, :],
                            in0=ptr[:].rearrange("p (a b) -> p a b", a=8),
                            in1=wf_tok[:, (blk - 2) * 8:(blk - 1) * 8].rearrange(
                                "p a -> p a ()").to_broadcast([128, 8, 128]),
                            op=ALU.mult), wait=aw("tensor"), inc=[(s_v, 1)])

                for hf in range(2):
                    h0 = hf * 16
                    cum_src = d_cum[h0:h0 + 16, cs].rearrange("h t -> () h t").to_broadcast(
                        [128, 16, 128])
                    dma("sync", cumrow[:].rearrange("p (h t) -> p h t", h=16), cum_src,
                        wait=aw("vector", "scalar", "tensor"))
                    p.step("scalar", lambda e: e.activation(
                        out=expcr[:], in_=cumrow[:], func=AF.Exp),
                        wait=aw("sync", "vector"), inc=[(s_s, 1)])
                    p.step("vector", lambda e, h0=h0: e.tensor_tensor(
                        out=cumrow[:].rearrange("p (a b) -> p a b", a=16),
                        in0=cumrow[:].rearrange("p (a b) -> p a b", a=16),
                        in1=ct_tok[:, h0:h0 + 16].rearrange("p a -> p a ()").to_broadcast(
                            [128, 16, 128]),
                        op=ALU.add), wait=aw("scalar"), inc=[(s_v, 1)])
                    p.step("vector", lambda e: e.tensor_tensor(
                        out=cumrow[:].rearrange("p (a b) -> p a b", a=16),
                        in0=cumrow[:].rearrange("p (a b) -> p a b", a=16),
                        in1=c_tm[:].rearrange("p b -> p () b").to_broadcast(
                            [128, 16, 128]),
                        op=ALU.add), inc=[(s_v, 1)])
                    p.step("scalar", lambda e: e.activation(
                        out=decay[:], in_=cumrow[:], func=AF.Exp),
                        wait=aw("vector"), inc=[(s_s, 1)])
                    p.step("vector", lambda e, h0=h0, cs=cs: e.tensor_tensor(
                        out=Ctil[:],
                        in0=C_fm[:, h0:h0 + 16, cs],
                        in1=expcr[:].rearrange("p (a b) -> p a b", a=16),
                        op=ALU.mult), wait=aw("scalar"), inc=[(s_v, 1)])
                    def _sc_mm(e, h0=h0, cs=cs):
                        for j in range(16):
                            inst = e.matmul(psc[:, j * 128:(j + 1) * 128],
                                            B_fm[:, h0 + j, cs], C_fm[:, h0 + j, cs],
                                            start=True, stop=True)
                        return inst
                    p.step("tensor", _sc_mm, wait=aw("vector", "scalar"),
                           inc=[(s_t, 1)])
                    p.step("vector", lambda e: e.tensor_tensor(
                        out=scoresT[:].rearrange("p a b -> p (a b)"),
                        in0=psc[:], in1=decay[:], op=ALU.mult),
                        wait=aw("tensor", "scalar"), inc=[(s_v, 1)])
                    p.step("vector", lambda e, h0=h0: e.tensor_copy(
                        out=Hbf[:].rearrange("p a b -> p (a b)"),
                        in_=carry[:, h0:h0 + 16, :].rearrange("p a b -> p (a b)")),
                        wait=aw("tensor"), inc=[(s_v, 1)])
                    def _y_mm(e, h0=h0, cs=cs):
                        for j in range(16):
                            h = h0 + j
                            xs = X_tok[:, h // 2, (h % 2) * 64:(h % 2) * 64 + 64]
                            prow = slice((j % 2) * 64, (j % 2) * 64 + 64)
                            pcol = slice((j // 2) * 128, (j // 2) * 128 + 128)
                            e.matmul(psc[prow, pcol], xs, scoresT[:, j, :],
                                     start=True, stop=False)
                            inst = e.matmul(psc[prow, pcol], Hbf[:, j, :],
                                            Ctil[:, j, :], start=False, stop=True)
                        return inst
                    p.step("tensor", _y_mm, wait=aw("vector"), inc=[(s_t, 1)])
                    def _s_mm(e, h0=h0):
                        for j in range(16):
                            h = h0 + j
                            xs = X_tok[:, h // 2, (h % 2) * 64:(h % 2) * 64 + 64]
                            inst = e.matmul(psc[:, 1024 + j * 64:1024 + (j + 1) * 64],
                                            B_tok[:, h, :], xs, start=True, stop=True)
                        return inst
                    p.step("tensor", _s_mm, inc=[(s_t, 1)])
                    p.step("vector", lambda e, h0=h0, cs=cs: e.tensor_tensor(
                        out=ydsk[:],
                        in0=X_fm[:, h0 // 2:h0 // 2 + 8, cs],
                        in1=w_dskip[:, h0 // 2:h0 // 2 + 8].rearrange(
                            "p a -> p a ()").to_broadcast([128, 8, 128]),
                        op=ALU.mult), wait=aw("tensor"), inc=[(s_v, 1)])
                    p.step("vector", lambda e: e.tensor_tensor(
                        out=y_g[:],
                        in0=psc[:, 0:1024].rearrange("p (a b) -> p a b", a=8),
                        in1=ydsk[:], op=ALU.add),
                        inc=[(s_v, 1)])
                    p.step("vector", lambda e, h0=h0, cs=cs: e.tensor_tensor(
                        out=y_g[:], in0=y_g[:],
                        in1=z_fm[:, h0 // 2:h0 // 2 + 8, cs],
                        op=ALU.mult), inc=[(s_v, 1)])
                    p.step("scalar", lambda e: e.activation(
                        out=ysq[:].rearrange("p a b -> p (a b)"),
                        in_=y_g[:].rearrange("p a b -> p (a b)"), func=AF.Square),
                        wait=aw("vector"), inc=[(s_s, 1)])
                    def _sq_mm(e, hf=hf, cc=cc):
                        for kch in range(8):
                            inst = e.matmul(pmA[:, 64 + cc:64 + cc + 1],
                                            ysq[:, kch, :], c_onesb[:],
                                            start=(hf == 0 and kch == 0),
                                            stop=(hf == 1 and kch == 7))
                        return inst
                    p.step("tensor", _sq_mm, wait=aw("scalar"), inc=[(s_t, 1)])
                    p.step("vector", lambda e, h0=h0, cs=cs: e.tensor_tensor(
                        out=y2_fm[:, h0 // 2:h0 // 2 + 8, cs],
                        in0=y_g[:],
                        in1=w_gnw[:, h0 // 2:h0 // 2 + 8].rearrange(
                            "p a -> p a ()").to_broadcast([128, 8, 128]),
                        op=ALU.mult), wait=aw("scalar"), inc=[(s_v, 1)])
                    p.step("vector", lambda e, h0=h0: e.tensor_tensor(
                        out=carry[:, h0:h0 + 16, :],
                        in0=carry[:, h0:h0 + 16, :],
                        in1=expcr[:].rearrange("p (a b) -> p a b", b=128)[:, :, 127:128].to_broadcast(
                            [128, 16, 64]),
                        op=ALU.mult), wait=aw("tensor"), inc=[(s_v, 1)])
                    p.step("vector", lambda e, h0=h0: e.tensor_tensor(
                        out=carry[:, h0:h0 + 16, :],
                        in0=carry[:, h0:h0 + 16, :],
                        in1=psc[:, 1024:2048].rearrange("p (a b) -> p a b", a=16),
                        op=ALU.add), inc=[(s_v, 1)])

                if g == 0 and cc == (1 if "chunk1" in DEBUG else 0):
                    for nm, t, sh, dt_ in (
                            ("scoresT", scoresT, (128, 16, 128), BF16),
                            ("decay", decay, (128, 2048), BF16),
                            ("X_tok", X_tok, (128, 16, 128), BF16),
                            ("B_tok", B_tok, (128, 32, 128), BF16),
                            ("Ctil", Ctil, (128, 16, 128), BF16),
                            ("Hbf", Hbf, (128, 16, 64), BF16)):
                        if dout(nm, sh, dt_) is not None:
                            dma("sync", dbg[nm], t[:], wait=aw("vector", "scalar"))

            if g == 0 and dout("y2c0", (128, 16, 512), BF16) is not None:
                dma("sync", dbg["y2c0"], y2_fm[:], wait=aw("vector", "scalar"))
            if g == 0 and "carry" in DEBUG:
                d = dout("carry", (128, 32, 64), F32, force=True)
                dma("sync", d, carry[:], wait=aw("vector", "scalar", "tensor"))

            for tc in range(NCHUNK):
                p.step("scalar", lambda e, tc=tc: e.activation(
                    out=rstd_c[:, tc:tc + 1], in_=pmA[:, 64 + tc:64 + tc + 1],
                    func=AF.Ln, bias=c_misc[:, 0:1], scale=1.0 / 2048),
                    wait=aw("tensor", "vector"), inc=[(s_s, 1)])
                p.step("scalar", lambda e, tc=tc: e.activation(
                    out=rstd_c[:, tc:tc + 1], in_=rstd_c[:, tc:tc + 1],
                    func=AF.Exp, scale=-0.5), wait=aw("scalar"), inc=[(s_s, 1)])

            for half in range(2):
                for kt in range(16):
                    buf = ipslab[kt % 2]
                    p.step("gpsimd", lambda e, buf=buf, kt=kt: e.dma_start(
                        out=buf[:].rearrange("p a b -> p (a b)"), in_=opj_d[kt]),
                        wait=aw("tensor"), inc=[(s_ax, 16)])
                    for tci in range(2):
                        tc = half * 2 + tci
                        pmm = pOP if tci == 0 else psc
                        def _op_mm(e, buf=buf, kt=kt, tc=tc, pmm=pmm):
                            bv = buf[:].rearrange("p a b -> p (a b)")
                            for nn in range(2):
                                inst = e.matmul(
                                    pmm[:, nn * 512:(nn + 1) * 512],
                                    y2_fm[:, kt, tc * 128:(tc + 1) * 128],
                                    bv[:, nn * 512:(nn + 1) * 512],
                                    start=(kt == 0), stop=(kt == 15))
                            return inst
                        p.step("tensor", _op_mm,
                               wait=aw("gpsimd2", "vector") if tci == 0 else (),
                               inc=[(s_t, 1)] if tci == 1 else ())
                for tci in range(2):
                    tc = half * 2 + tci
                    row = t0 + tc * CH
                    pmm = pOP if tci == 0 else psc
                    dma("sync", x_t[:], x_d[row:row + 128, :],
                        wait=aw("vector", "scalar", "tensor"))
                    p.step("vector", lambda e, tc=tc: e.scalar_tensor_tensor(
                        out=tmpc[:, 2:3], in0=mcol[:, tc:tc + 1], scalar=-1.0,
                        in1=rsq1[:, tc:tc + 1], op0=ALU.mult, op1=ALU.mult),
                        wait=aw("sync", "vector"), inc=[(s_v, 1)])
                    p.step("scalar", lambda e, tc=tc: e.activation(
                        out=u_t[:], in_=x_t[:], func=AF.Identity, bias=tmpc[:, 2:3],
                        scale=rsq1[:, tc:tc + 1]), wait=aw("vector"), inc=[(s_s, 1)])
                    p.step("vector", lambda e, tc=tc, pmm=pmm: e.scalar_tensor_tensor(
                        out=ao_t[:], in0=pmm[:, 0:1024], scalar=rstd_c[:, tc:tc + 1],
                        in1=u_t[:], op0=ALU.mult, op1=ALU.add),
                        wait=aw("scalar", "tensor"), inc=[(s_v, 1)])
                    if g == 0 and tc == 0 and "aodbg" in DEBUG:
                        d = dout("ao0", (128, 1024), F32, force=True)
                        dma("sync", d, ao_t[:], wait=aw("vector"))
                        d = dout("u0", (128, 1024), F32, force=True)
                        dma("sync", d, u_t[:], wait=aw("vector"))
                        d = dout("rstd", (128, 4), F32, force=True)
                        dma("sync", d, rstd_c[:], wait=aw("vector", "scalar"))
                    ln_stats(ao_t[:], tmpc[:, 3:4], tmpc[:, 4:5])
                    rsq_from_var(tmpc[:, 4:5], tmpc[:, 5:6])
                    p.step("vector", lambda e: e.scalar_tensor_tensor(
                        out=tmpc[:, 6:7], in0=tmpc[:, 3:4], scalar=-1.0,
                        in1=tmpc[:, 5:6], op0=ALU.mult, op1=ALU.mult),
                        wait=aw("scalar", "vector"), inc=[(s_v, 1)])
                    p.step("scalar", lambda e: e.activation(
                        out=x_t[:], in_=x_t[:], func=AF.Identity, bias=tmpc[:, 6:7]),
                        wait=aw("vector"), inc=[(s_s, 1)])
                    p.step("vector", lambda e: e.scalar_tensor_tensor(
                        out=x_t[:], in0=ao_t[:], scalar=tmpc[:, 5:6],
                        in1=x_t[:], op0=ALU.mult, op1=ALU.add),
                        wait=aw("scalar"), inc=[(s_v, 1)])
                    dma("sync", d_x1[row:row + 128, :], x_t[:], wait=aw("vector"))

        m_st.close()

    if "x1" in DEBUG:
        d = dout("x1", (L, D))
        dma("sync", d, d_x1, wait=aw("vector", "scalar", "tensor", "sync"))

    hid_t = stack.enter_context(nc.sbuf_tensor("sb_hid_t", [128, 8, 128], BF16))
    fa_st, sb = arena()
    wq_sb = sb("wq_sb", (128, 8, 512), BF16)
    keys_sb = sb("keys_sb", (128, 512), BF16)
    q_bf = sb("q_bf", (128, 4, 128), BF16)
    sim_sb = sb("sim_sb", (128, 512), F32)
    mx8 = sb("mx8", (128, 2, 4, 8), F32)
    ixu = sb("ixu", (128, 2, 4, 8), U32)
    ixf = sb("ixf", (128, 2, 4, 8), F32)
    cmb = sb("cmb", (128, 4, 64), F32)
    v8h = sb("v8h", (128, 4, 8), F32)
    c8u = sb("c8u", (128, 4, 8), U32)
    c8f = sb("c8f", (128, 4, 8), F32)
    j8 = sb("j8", (128, 4, 8), F32)
    i8 = sb("i8", (128, 4, 8), F32)
    oh8 = sb("oh8", (128, 4, 8, 8), F32)
    ixg = sb("ixg", (128, 4, 8), F32)
    iyg = sb("iyg", (128, 4, 8), F32)
    ec8 = sb("ec8", (128, 4, 8), F32)
    mq = sb("mq", (128, 4, 8), F32)
    eq = sb("eq", (128, 4, 8), F32)
    p8 = sb("p8", (128, 4, 8), F32)
    den = sb("den", (128, 4), F32)
    pc_bf = sb("pc_bf", (128, 32), BF16)
    ix_i16 = sb("ix_i16", (128, 96), I16)

    if DO_FFN_A:
        dma("gpsimd", wq_sb[:], wq_d, wait=aw("tensor", "vector", "scalar"))
        dma("gpsimd", keys_sb[:], keys_d)
        for ti in range(N_FFN_TILES):
            row = ti * 128
            dma("sync", x_t[:], d_x1[row:row + 128, :],
                wait=aw("vector", "scalar", "tensor"))
            ln_stats(x_t[:], mcol[:, 0:1], vcol[:, 0:1])
            rsq_from_var(vcol[:, 0:1], rsq1[:, 0:1])
            p.step("vector", lambda e: e.scalar_tensor_tensor(
                out=tmpc[:, 1:2], in0=mcol[:, 0:1], scalar=-1.0,
                in1=rsq1[:, 0:1], op0=ALU.mult, op1=ALU.mult),
                wait=aw("scalar"), inc=[(s_v, 1)])
            p.step("scalar", lambda e: e.activation(
                out=tb_t[:], in_=x_t[:], func=AF.Identity, bias=tmpc[:, 1:2],
                scale=rsq1[:, 0:1]), wait=aw("vector", "tensor"), inc=[(s_s, 1)])
            for kt in range(8):
                p.step("tensor", lambda e, kt=kt: e.transpose(
                    ptr[:, kt * 128:(kt + 1) * 128],
                    tb_t[:, kt * 128:(kt + 1) * 128], c_id[:]),
                    wait=aw("scalar", "vector") if kt == 0 else (),
                    inc=[(s_t, 1)] if kt == 7 else ())
            p.step("vector", lambda e: e.tensor_copy(
                out=hid_t[:].rearrange("p a b -> p (a b)"), in_=ptr[:]),
                wait=aw("tensor"), inc=[(s_v, 1)])
            dma("sync", d_hid[ti].rearrange("k d t -> d k t"), hid_t[:], wait=aw("vector"))
            if FA_CUT <= 1:
                continue
            def _q_mm(e):
                for qc in range(4):
                    for kt in range(8):
                        inst = e.matmul(pmA[:, qc * 128:(qc + 1) * 128],
                                        wq_sb[:, kt, qc * 128:(qc + 1) * 128],
                                        hid_t[:, kt, :],
                                        start=(kt == 0), stop=(kt == 7))
                return inst
            p.step("tensor", _q_mm, wait=aw("vector", "gpsimd2"), inc=[(s_t, 1)])
            p.step("vector", lambda e: e.tensor_copy(
                out=q_bf[:].rearrange("p a b -> p (a b)"), in_=pmA[:]),
                wait=aw("tensor"), inc=[(s_v, 1)])
            if FA_CUT <= 2:
                continue
            def _sim_mm(e):
                for blk in range(8):
                    p_, h = blk // 4, blk % 4
                    qc = (p_ * 256 + h * 64) // 128
                    inst = e.matmul(pmA[:, blk * 64:(blk + 1) * 64],
                                    q_bf[:, qc, :],
                                    keys_sb[:, blk * 64:(blk + 1) * 64],
                                    start=True, stop=True)
                return inst
            p.step("tensor", _sim_mm, wait=aw("vector"), inc=[(s_t, 1)])
            p.step("vector", lambda e: e.tensor_copy(out=sim_sb[:], in_=pmA[:]),
                   wait=aw("tensor"), inc=[(s_v, 1)])
            if FA_CUT <= 3:
                continue
            def _max8(e):
                for blk in range(8):
                    p_, h = blk // 4, blk % 4
                    inst = e.max(out=mx8[:, p_, h, :],
                                 in_=sim_sb[:, blk * 64:(blk + 1) * 64])
                return inst
            p.step("vector", _max8, inc=[(s_v, 1)])
            def _mi8(e):
                for blk in range(8):
                    p_, h = blk // 4, blk % 4
                    inst = e.max_index(out=ixu[:, p_, h, :],
                                       in_max=mx8[:, p_, h, :],
                                       in_values=sim_sb[:, blk * 64:(blk + 1) * 64])
                return inst
            p.step("vector", _mi8, wait=aw("vector"), inc=[(s_v, 1)])
            p.step("vector", lambda e: e.tensor_tensor(
                out=ixf[:].rearrange("p a b c -> p (a b c)"),
                in0=ixu[:].rearrange("p a b c -> p (a b c)"),
                in1=c_ones[:, 0:64], op=ALU.mult),
                wait=aw("vector"), inc=[(s_v, 1)])
            if FA_CUT <= 4:
                continue
            p.step("vector", lambda e: e.tensor_tensor(
                out=cmb[:].rearrange("p h (i j) -> p h i j", j=8),
                in0=mx8[:, 0, :, :].rearrange("p h i -> p h i ()").to_broadcast([128, 4, 8, 8]),
                in1=mx8[:, 1, :, :].rearrange("p h j -> p h () j").to_broadcast([128, 4, 8, 8]),
                op=ALU.add), wait=aw("vector"), inc=[(s_v, 1)])
            def _v8(e):
                for h in range(4):
                    inst = e.max(out=v8h[:, h, :], in_=cmb[:, h, :])
                return inst
            p.step("vector", _v8, wait=aw("vector"), inc=[(s_v, 1)])
            def _c8(e):
                for h in range(4):
                    inst = e.max_index(out=c8u[:, h, :], in_max=v8h[:, h, :],
                                       in_values=cmb[:, h, :])
                return inst
            p.step("vector", _c8, wait=aw("vector"), inc=[(s_v, 1)])
            p.step("vector", lambda e: e.tensor_tensor(
                out=c8f[:].rearrange("p a b -> p (a b)"),
                in0=c8u[:].rearrange("p a b -> p (a b)"),
                in1=c_ones[:, 0:32], op=ALU.mult),
                wait=aw("vector"), inc=[(s_v, 1)])
            for h in range(4):
                p.step("vector", lambda e, h=h: e.tensor_tensor(
                    out=oh8[:, h, :, 0:7],
                    in0=c8f[:, h, :].rearrange("p k -> p k ()").to_broadcast([128, 8, 7]),
                    in1=c_iota8[:, 8:15].rearrange("p m -> p () m").to_broadcast([128, 8, 7]),
                    op=ALU.is_ge), wait=aw("vector"), inc=[(s_v, 1)])
                p.step("vector", lambda e, h=h: e.tensor_reduce(
                    out=i8[:, h, :], in_=oh8[:, h, :, 0:7], axis=AXL.X, op=ALU.add),
                    wait=aw("vector"), inc=[(s_v, 1)])
            p.step("vector", lambda e: e.scalar_tensor_tensor(
                out=j8[:].rearrange("p a b -> p (a b)"),
                in0=i8[:].rearrange("p a b -> p (a b)"), scalar=-8.0,
                in1=c8f[:].rearrange("p a b -> p (a b)"),
                op0=ALU.mult, op1=ALU.add), wait=aw("vector"), inc=[(s_v, 1)])
            if ti == 0 and "fadbg" in DEBUG:
                d = dout("j8e", (128, 4, 8), F32, force=True)
                dma("sync", d, j8[:], wait=aw("vector"))
                d = dout("i8e", (128, 4, 8), F32, force=True)
                dma("sync", d, i8[:], wait=aw("vector"))
            if FA_CUT <= 5:
                continue
            for (sel, src_p, dst) in ((i8, 0, ixg), (j8, 1, iyg)):
                for h in range(4):
                    p.step("vector", lambda e, sel=sel, h=h: e.tensor_tensor(
                        out=oh8[:, h, :, :],
                        in0=sel[:, h, :].rearrange("p k -> p k ()").to_broadcast([128, 8, 8]),
                        in1=c_iota8[:, 0:8].rearrange("p m -> p () m").to_broadcast([128, 8, 8]),
                        op=ALU.is_equal), wait=aw("vector"), inc=[(s_v, 1)])
                    p.step("vector", lambda e, src_p=src_p, h=h: e.tensor_tensor(
                        out=oh8[:, h, :, :], in0=oh8[:, h, :, :],
                        in1=ixf[:, src_p, h, :].rearrange("p m -> p () m").to_broadcast(
                            [128, 8, 8]), op=ALU.mult), wait=aw("vector"), inc=[(s_v, 1)])
                    p.step("vector", lambda e, dst=dst, h=h: e.tensor_reduce(
                        out=dst[:, h, :], in_=oh8[:, h, :, :], axis=AXL.X, op=ALU.add),
                        wait=aw("vector"), inc=[(s_v, 1)])
            p.step("vector", lambda e: e.scalar_tensor_tensor(
                out=ec8[:].rearrange("p a b -> p (a b)"),
                in0=ixg[:].rearrange("p a b -> p (a b)"), scalar=64.0,
                in1=iyg[:].rearrange("p a b -> p (a b)"),
                op0=ALU.mult, op1=ALU.add), wait=aw("vector"), inc=[(s_v, 1)])
            if ti == 0 and "fadbg" in DEBUG:
                for nm, t, sh in (("i8d", i8, (128, 4, 8)), ("j8d", j8, (128, 4, 8)),
                                  ("ixgd", ixg, (128, 4, 8)), ("iygd", iyg, (128, 4, 8)),
                                  ("ec8d", ec8, (128, 4, 8)), ("c8d", c8f, (128, 4, 8))):
                    d = dout(nm, sh, F32, force=True)
                    dma("sync", d, t[:], wait=aw("vector", "scalar"))
            if FA_CUT <= 6:
                continue
            for ri, (lo, hi) in enumerate(((0, 1366), (1366, 2732), (2732, 4096))):
                p.step("vector", lambda e, lo=lo: e.scalar_tensor_tensor(
                    out=mq[:].rearrange("p a b -> p (a b)"),
                    in0=ec8[:].rearrange("p a b -> p (a b)"), scalar=float(lo),
                    in1=c_ones[:, 0:32], op0=ALU.is_ge, op1=ALU.mult),
                    wait=aw("vector"), inc=[(s_v, 1)])
                p.step("vector", lambda e, hi=hi: e.scalar_tensor_tensor(
                    out=eq[:].rearrange("p a b -> p (a b)"),
                    in0=ec8[:].rearrange("p a b -> p (a b)"), scalar=float(hi),
                    in1=c_ones[:, 0:32], op0=ALU.is_lt, op1=ALU.mult),
                    wait=aw("vector"), inc=[(s_v, 1)])
                p.step("vector", lambda e: e.tensor_tensor(
                    out=mq[:].rearrange("p a b -> p (a b)"),
                    in0=mq[:].rearrange("p a b -> p (a b)"),
                    in1=eq[:].rearrange("p a b -> p (a b)"), op=ALU.mult),
                    inc=[(s_v, 1)])
                p.step("vector", lambda e, lo=lo: e.scalar_tensor_tensor(
                    out=eq[:].rearrange("p a b -> p (a b)"),
                    in0=ec8[:].rearrange("p a b -> p (a b)"), scalar=float(-lo),
                    in1=mq[:].rearrange("p a b -> p (a b)"), op0=ALU.add, op1=ALU.mult),
                    wait=aw("vector"), inc=[(s_v, 1)])
                p.step("vector", lambda e: e.tensor_tensor(
                    out=eq[:].rearrange("p a b -> p (a b)"),
                    in0=eq[:].rearrange("p a b -> p (a b)"),
                    in1=mq[:].rearrange("p a b -> p (a b)"), op=ALU.add),
                    wait=aw("vector"), inc=[(s_v, 1)])
                p.step("vector", lambda e, ri=ri: e.scalar_tensor_tensor(
                    out=ix_i16[:, ri * 32:(ri + 1) * 32],
                    in0=eq[:].rearrange("p a b -> p (a b)"), scalar=-1.0,
                    in1=c_ones[:, 0:32], op0=ALU.add, op1=ALU.mult),
                    wait=aw("vector"), inc=[(s_v, 1)])
            if FA_CUT <= 7:
                continue
            p.step("vector", lambda e: e.tensor_tensor(
                out=p8[:], in0=v8h[:],
                in1=v8h[:, :, 0:1].to_broadcast([128, 4, 8]), op=ALU.subtract),
                wait=aw("vector"), inc=[(s_v, 1)])
            p.step("scalar", lambda e: e.activation(
                out=p8[:].rearrange("p a b -> p (a b)"),
                in_=p8[:].rearrange("p a b -> p (a b)"), func=AF.Exp),
                wait=aw("vector"), inc=[(s_s, 1)])
            p.step("vector", lambda e: e.tensor_reduce(
                out=den[:], in_=p8[:], axis=AXL.X, op=ALU.add),
                wait=aw("scalar"), inc=[(s_v, 1)])
            p.step("vector", lambda e: e.reciprocal(out=den[:], in_=den[:]),
                   wait=aw("vector"), inc=[(s_v, 1)])
            p.step("vector", lambda e: e.tensor_tensor(
                out=pc_bf[:].rearrange("p (a b) -> p a b", a=4),
                in0=p8[:],
                in1=den[:].rearrange("p a -> p a ()").to_broadcast([128, 4, 8]),
                op=ALU.mult), wait=aw("vector"), inc=[(s_v, 1)])
            dma("sync", d_pc[ti], pc_bf[:], wait=aw("vector"))
            dma("sync", d_ix[ti], ix_i16[:], wait=aw("vector"))
            if ti == 0:
                for nm, t, sh, dt_ in (("sim", sim_sb, (128, 512), F32),
                                       ("mx8", mx8, (128, 2, 4, 8), F32),
                                       ("ixf", ixf, (128, 2, 4, 8), F32),
                                       ("pc0", pc_bf, (128, 32), BF16),
                                       ("ix0", ix_i16, (128, 96), I16)):
                    if dout(nm, sh, dt_) is not None:
                        dma("sync", dbg[nm], t[:], wait=aw("vector"))

    fa_st.close()

    if DO_FFN_B:
        fb_st, sb = arena()
        wup_sb = sb("wup_sb", (128, 8, 4096), BF16)
        hidH = sb("hidH", (128, 8, 2048), BF16)
        up_t = sb("up_t", (128, 2048), BF16)
        dma("gpsimd", wup_sb[:], wup_d, wait=aw("tensor", "vector", "scalar"))
        for half in range(2):
            hh = half * 2048
            for a_ in range(16):
                dma("sync", hidH[:, :, a_ * 128:(a_ + 1) * 128],
                    d_hid[half * 16 + a_].rearrange("k d t -> d k t"),
                    wait=aw("tensor") if a_ == 0 else ())
            for ic in range(32):
                def _up_mm(e, ic=ic):
                    for kt in range(8):
                        for gs in range(4):
                            inst = e.matmul(psc[:, gs * 512:(gs + 1) * 512],
                                            wup_sb[:, kt, ic * 128:(ic + 1) * 128],
                                            hidH[:, kt, gs * 512:(gs + 1) * 512],
                                            start=(kt == 0), stop=(kt == 7))
                    return inst
                p.step("tensor", _up_mm,
                       wait=aw("sync", "gpsimd2") if ic == 0 else aw("scalar"),
                       inc=[(s_t, 1)])
                p.step("scalar", lambda e: e.activation(
                    out=up_t[:], in_=psc[:], func=AF.Silu),
                    wait=aw("tensor", "sync"), inc=[(s_s, 1)])
                dma("sync", d_up[ic][:, hh:hh + 2048], up_t[:], wait=aw("scalar"))
        fb_st.close()
        fb2_st, sb = arena()
        wdn_sb = sb("wdn_sb", (128, 32, D), BF16)
        upt_sb = sb("upt_sb", (128, 32, 128), BF16)
        mlp_t = sb("mlp_t", (128, 1024), BF16)
        dma("gpsimd", wdn_sb[:], wdn_d, wait=aw("tensor", "scalar", "sync"))
        for ti in range(N_FFN_TILES):
            src = d_up[:, :, ti * 128:(ti + 1) * 128].rearrange("a d t -> d a t")
            dma("sync", upt_sb[:], src, wait=aw("tensor", "vector"))
            def _dn_mm(e, ti=ti):
                for ic in range(32):
                    for nn in range(2):
                        inst = e.matmul(pOP[:, nn * 512:(nn + 1) * 512],
                                        upt_sb[:, ic, :],
                                        wdn_sb[:, ic, nn * 512:(nn + 1) * 512],
                                        start=(ic == 0), stop=(ic == 31))
                return inst
            p.step("tensor", _dn_mm, wait=aw("sync", "gpsimd2", "vector"),
                   inc=[(s_t, 1)])
            p.step("vector", lambda e: e.tensor_copy(out=mlp_t[:], in_=pOP[:]),
                   wait=aw("tensor", "sync"), inc=[(s_v, 1)])
            dma("sync", d_mlp[ti * 128:(ti + 1) * 128, :], mlp_t[:],
                wait=aw("vector"))

    if DO_FFN_B:
        fb2_st.close()
    if DO_FFN_C:
        fc_st, sb = arena()
        det_sb = sb("det_sb", (128, 8, 4096), BF16)
        ue_sb = sb("ue_sb", (128, 32, D), BF16)
        S_bf = sb("S_bf", (128, 4096), BF16)
        P_h = sb("P_h", (128, 4096), BF16)
        Wsil = sb("Wsil", (128, 4096), BF16)
        Wacc = sb("Wacc", (128, 4096), BF16)
        WT = sb("WT", (128, 32, 128), BF16)
        pc_t = sb("pc_t", (128, 32), BF16)
        ix_t = sb("ix_t", (128, 96), I16)
        mlp2_t = sb("mlp2_t", (128, 1024), BF16)
        dma("gpsimd", det_sb[:], det_d, wait=aw("tensor", "vector", "scalar"))
        dma("gpsimd", ue_sb[:], ue_d)
        for ti in range(N_FFN_TILES):
            row = ti * 128
            dma("sync", hid_t[:], d_hid[ti].rearrange("k d t -> d k t"), wait=aw("tensor", "vector", "scalar"))
            dma("sync", pc_t[:], d_pc[ti], wait=aw("gpsimd"))
            dma("sync", ix_t[:], d_ix[ti], wait=aw("gpsimd"))
            dma("sync", mlp2_t[:], d_mlp[row:row + 128, :], wait=aw("vector"))
            dma("sync", x_t[:], d_x1[row:row + 128, :], wait=aw("vector", "scalar"))
            for r in range(2):
                def _s_mm(e, r=r):
                    for kt in range(8):
                        for nn in range(4):
                            inst = e.matmul(psc[:, nn * 512:(nn + 1) * 512],
                                            hid_t[:, kt, :],
                                            det_sb[:, kt, r * 2048 + nn * 512:
                                                   r * 2048 + (nn + 1) * 512],
                                            start=(kt == 0), stop=(kt == 7))
                    return inst
                p.step("tensor", _s_mm,
                       wait=aw("sync", "gpsimd2") if r == 0 else aw("vector"),
                       inc=[(s_t, 1)])
                p.step("vector", lambda e, r=r: e.tensor_copy(
                    out=S_bf[:, r * 2048:(r + 1) * 2048], in_=psc[:]),
                    wait=aw("tensor"), inc=[(s_v, 1)])
            for h in range(4):
                for ri, (lo, hi) in enumerate(((0, 1366), (1366, 2732), (2732, 4096))):
                    p.step("gpsimd", lambda e, h=h, ri=ri, lo=lo, hi=hi: e.local_scatter(
                        out_ap=P_h[:, lo:hi], data_ap=pc_t[:, h * 8:(h + 1) * 8],
                        idxs_ap=ix_t[:, ri * 32 + h * 8:ri * 32 + (h + 1) * 8],
                        channels=128, num_elems=hi - lo, num_idxs=8),
                        wait=aw("sync", "vector") if ri == 0 else (),
                        inc=[(s_g, 1)])
                p.step("vector", lambda e: e.tensor_tensor(
                    out=Wsil[:], in0=S_bf[:], in1=P_h[:], op=ALU.mult),
                    wait=aw("gpsimd", "scalar"), inc=[(s_v, 1)])
                p.step("scalar", lambda e, h=h: e.activation(
                    out=(Wacc[:] if h == 0 else Wsil[:]), in_=Wsil[:], func=AF.Silu),
                    wait=aw("vector"), inc=[(s_s, 1)])
                if h > 0:
                    p.step("vector", lambda e: e.tensor_tensor(
                        out=Wacc[:], in0=Wacc[:], in1=Wsil[:], op=ALU.add),
                        wait=aw("scalar"), inc=[(s_v, 1)])
            for bq in range(4):
                def _wt_tr(e, bq=bq):
                    for j in range(8):
                        ecn = bq * 8 + j
                        inst = e.transpose(ptr[:, j * 128:(j + 1) * 128],
                                           Wacc[:, ecn * 128:(ecn + 1) * 128], c_id[:])
                    return inst
                p.step("tensor", _wt_tr, wait=aw("vector", "scalar"), inc=[(s_t, 1)])
                p.step("vector", lambda e, bq=bq: e.tensor_copy(
                    out=WT[:, bq * 8:(bq + 1) * 8, :].rearrange("p a b -> p (a b)"),
                    in_=ptr[:]), wait=aw("tensor"), inc=[(s_v, 1)])
            def _ue_mm(e):
                for ec in range(32):
                    for nn in range(2):
                        inst = e.matmul(pOP[:, nn * 512:(nn + 1) * 512],
                                        WT[:, ec, :],
                                        ue_sb[:, ec, nn * 512:(nn + 1) * 512],
                                        start=(ec == 0), stop=(ec == 31))
                return inst
            p.step("tensor", _ue_mm, wait=aw("vector"), inc=[(s_t, 1)])
            p.step("vector", lambda e: e.tensor_tensor(
                out=ao_t[:], in0=pOP[:], in1=mlp2_t[:], op=ALU.add),
                wait=aw("tensor", "sync"), inc=[(s_v, 1)])
            ln_stats(ao_t[:], tmpc[:, 3:4], tmpc[:, 4:5])
            rsq_from_var(tmpc[:, 4:5], tmpc[:, 5:6])
            p.step("vector", lambda e: e.scalar_tensor_tensor(
                out=tmpc[:, 6:7], in0=tmpc[:, 3:4], scalar=-1.0,
                in1=tmpc[:, 5:6], op0=ALU.mult, op1=ALU.mult),
                wait=aw("scalar"), inc=[(s_v, 1)])
            p.step("scalar", lambda e: e.activation(
                out=x_t[:], in_=x_t[:], func=AF.Identity, bias=tmpc[:, 6:7]),
                wait=aw("vector"), inc=[(s_s, 1)])
            p.step("vector", lambda e: e.scalar_tensor_tensor(
                out=x_t[:], in0=ao_t[:], scalar=tmpc[:, 5:6],
                in1=x_t[:], op0=ALU.mult, op1=ALU.add),
                wait=aw("scalar"), inc=[(s_v, 1)])
            dma("sync", out_d[row:row + 128, :], x_t[:], wait=aw("vector"))
            if ti == 0:
                for nm, t, sh, dt_ in (("S_bf", S_bf, (128, 4096), BF16),
                                       ("Wacc", Wacc, (128, 4096), BF16)):
                    if dout(nm, sh, dt_) is not None:
                        dma("sync", dbg[nm], t[:], wait=aw("vector", "scalar"))

    p.step("gpsimd", lambda e: e.nop(), wait=aw("sync", "gpsimd2"))

    p.replay()
    mybir.codegen_inst_isa_subclasses(nc)
    return nc, p.n_inst


def _prep_inputs(inp):
    f32 = np.float32
    ipw = np.asarray(inp["L0_attn_in_proj_w"], f32)
    ipw = np.pad(ipw, ((0, 96), (0, 0)))
    ipj = np.ascontiguousarray(ipw.reshape(NCH_IP, 128, D).transpose(0, 2, 1))
    convw = np.ascontiguousarray(
        np.asarray(inp["L0_attn_conv_w"], f32).reshape(80, 128, 4).transpose(1, 0, 2))
    convb = np.ascontiguousarray(
        np.asarray(inp["L0_attn_conv_b"], f32).reshape(80, 128).T)
    dtb = np.asarray(inp["L0_attn_dt_bias"], f32).reshape(32, 1)
    alog = np.asarray(inp["L0_attn_A_log"], f32).reshape(32, 1)
    dskip = np.ascontiguousarray(
        np.repeat(np.asarray(inp["L0_attn_D_skip"], f32), 64).reshape(16, 128).T)
    gnw = np.ascontiguousarray(
        np.asarray(inp["L0_attn_gnorm_w"], f32).reshape(16, 128).T)
    normw = np.ascontiguousarray(
        np.asarray(inp["L0_attn_norm_w"], f32).reshape(8, 128).T)
    opj = np.ascontiguousarray(
        np.asarray(inp["L0_attn_out_proj_w"], f32).T.reshape(16, 128, D))
    wq = np.ascontiguousarray(
        np.asarray(inp["L0_ffn_W_q"], f32).T.reshape(8, 128, 512).transpose(1, 0, 2))
    keys0 = np.asarray(inp["L0_ffn_keys"], f32).transpose(3, 2, 0, 1).reshape(64, 512)
    keys = np.zeros((128, 512), f32)
    for blk in range(8):
        off = (blk // 4 * 256 + (blk % 4) * 64) % 128
        keys[off:off + 64, blk * 64:(blk + 1) * 64] = keys0[:, blk * 64:(blk + 1) * 64]
    wup = np.ascontiguousarray(
        np.asarray(inp["L0_ffn_W_up"], f32).T.reshape(8, 128, 4096).transpose(1, 0, 2))
    wdn = np.ascontiguousarray(
        np.asarray(inp["L0_ffn_W_down"], f32).T.reshape(32, 128, D).transpose(1, 0, 2))
    det = np.ascontiguousarray(
        np.asarray(inp["L0_ffn_down_embed"], f32).T.reshape(8, 128, 4096).transpose(1, 0, 2))
    ue = np.ascontiguousarray(
        np.asarray(inp["L0_ffn_up_embed"], f32).reshape(32, 128, D).transpose(1, 0, 2))

    ident = np.eye(128, dtype=ml_dtypes.bfloat16)
    tm = np.where(np.arange(128)[None, :] >= np.arange(128)[:, None],
                  0.0, NEG).astype(f32)
    ones = np.ones((128, 512), f32)
    onesb = np.ones((128, 1), ml_dtypes.bfloat16)
    misc = np.tile(np.array([EPS, 1e-10, 1.0, 0.0], f32), (128, 1))
    identf = np.eye(128, dtype=f32)
    iota8 = np.tile(np.concatenate([np.arange(8), np.arange(8, 64, 8), [0.0]]).astype(f32),
                    (128, 1))

    w = dict(w_ipj=ipj, w_convw=convw, w_convb=convb, w_dtb=dtb, w_alog=alog,
             w_dskip=dskip, w_gnw=gnw, w_normw=normw, w_opj=opj, w_wq=wq,
             w_keys=keys, w_wup=wup, w_wdn=wdn, w_det=det, w_ue=ue,
             c_ident=ident, c_tm=tm, c_ones=ones, c_onesb=onesb, c_misc=misc, c_identf=identf, c_iota8=iota8)
    return w


def kernel(**inputs):
    w = _prep_inputs(inputs)
    x = np.asarray(inputs["L0_attn_x"], np.float32)
    nc, _ = build_nc()
    in_maps = []
    for b in range(8):
        m = dict(w)
        m["x"] = np.ascontiguousarray(x[b])
        in_maps.append(m)
    res = run_bass_kernel_spmd(nc, in_maps, core_ids=list(range(8)))
    return np.stack([r["out"] for r in res.results], axis=0)


# version 02
# speedup vs baseline: 1.2914x; 1.2914x over previous
import numpy as np
import ml_dtypes
import concourse.bass as bass
import concourse.mybir as mybir
from concourse import library_config
from concourse.bass_utils import run_bass_kernel_spmd
from contextlib import ExitStack

F32 = mybir.dt.float32
BF16 = mybir.dt.bfloat16
I16 = mybir.dt.int16
U32 = mybir.dt.uint32
AF = mybir.ActivationFunctionType
ALU = mybir.AluOpType
AXL = mybir.AxisListType

L = 4096
D = 1024
NCH_IP = 97
CH = 128
TG = 512
NCHUNK = TG // CH
NTILE = L // 128
NEG = -30000.0
EPS = 1e-5

DEBUG = set()
N_GROUPS = L // TG
DO_MAMBA = True
DO_FFN_A = True
DO_FFN_B = True
DO_FFN_C = True
N_FFN_TILES = NTILE
FA_CUT = 99


class Sem:
    __slots__ = ("h", "n", "name")

    def __init__(self, h, name):
        self.h = h
        self.n = 0
        self.name = name


class Prog:
    ENGINES = ("sync", "vector", "scalar", "tensor", "gpsimd")

    def __init__(self, nc, stack):
        self.nc = nc
        self.stack = stack
        self.prog = {e: [] for e in self.ENGINES}
        self.n_inst = 0

    def sem(self, name):
        return Sem(self.stack.enter_context(self.nc.semaphore(name)), name)

    def step(self, eng, fn, wait=(), inc=()):
        wv = [(s, s.n if v is None else v) for (s, v) in wait]
        for s, k in inc:
            s.n += k
        self.n_inst += 1

        def run(engine, wv=wv, inc=tuple(inc), fn=fn):
            for s, v in wv:
                if v > 0:
                    engine.wait_ge(s.h, v)
            inst = fn(engine)
            for s, k in inc:
                inst = inst.then_inc(s.h, k)

        self.prog[eng].append(run)

    def replay(self):
        nc = self.nc
        with nc.Block() as block:
            @block.sync
            def _(eng):
                for f in self.prog["sync"]:
                    f(eng)

            @block.vector
            def _(eng):
                for f in self.prog["vector"]:
                    f(eng)

            @block.scalar
            def _(eng):
                for f in self.prog["scalar"]:
                    f(eng)

            @block.tensor
            def _(eng):
                for f in self.prog["tensor"]:
                    f(eng)

            @block.gpsimd
            def _(eng):
                for f in self.prog["gpsimd"]:
                    f(eng)


def build_nc():
    nc = bass.Bass("TRN2", target_bir_lowering=False)
    stack = ExitStack()
    p = Prog(nc, stack)

    def din(name, shape, dtype=F32):
        return nc.dram_tensor(name, list(shape), dtype, kind="ExternalInput").ap()

    x_d = din("x", (L, D))
    ipj_d = din("w_ipj", (NCH_IP, D, 128))
    convw_d = din("w_convw", (128, 80, 4))
    convb_d = din("w_convb", (128, 80))
    dtb_d = din("w_dtb", (32, 1))
    alog_d = din("w_alog", (32, 1))
    dskip_d = din("w_dskip", (128, 16))
    gnw_d = din("w_gnw", (128, 16))
    normw_d = din("w_normw", (128, 8))
    opj_d = din("w_opj", (16, 128, D))
    wq_d = din("w_wq", (128, 8, 512))
    keys_d = din("w_keys", (128, 512))
    wup_d = din("w_wup", (128, 8, 4096))
    wdn_d = din("w_wdn", (128, 32, D))
    det_d = din("w_det", (128, 8, 4096))
    ue_d = din("w_ue", (128, 32, D))
    ident_d = din("c_ident", (128, 128), BF16)
    tm_d = din("c_tm", (128, 128))
    ones_d = din("c_ones", (128, 512))
    misc_d = din("c_misc", (128, 4))
    identf_d = din("c_identf", (128, 128))
    iota8_d = din("c_iota8", (128, 16))
    onesb_d = din("c_onesb", (128, 1), BF16)

    out_d = nc.dram_tensor("out", [L, D], F32, kind="ExternalOutput").ap()

    dbg = {}

    def dout(name, shape, dtype=F32, force=False):
        if (name in DEBUG or force) and name not in dbg:
            dbg[name] = nc.dram_tensor("dbg_" + name, list(shape), dtype,
                                       kind="ExternalOutput").ap()
        return dbg.get(name)

    d_ipbf = nc.dram_tensor("d_ipbf", [NCH_IP, 128, 8, 128], BF16, kind="Internal").ap()
    d_x1 = nc.dram_tensor("d_x1", [L, D], F32, kind="Internal").ap()
    d_cum = nc.dram_tensor("d_cum", [32, TG], F32, kind="Internal").ap()
    d_hid = nc.dram_tensor("d_hid", [NTILE, 8, 128, 128], BF16, kind="Internal").ap()
    d_up = nc.dram_tensor("d_up", [32, 128, L], BF16, kind="Internal").ap()
    d_mlp = nc.dram_tensor("d_mlp", [L, D], BF16, kind="Internal").ap()
    d_pc = nc.dram_tensor("d_pc", [NTILE, 128, 32], BF16, kind="Internal").ap()
    d_ix = nc.dram_tensor("d_ix", [NTILE, 128, 96], I16, kind="Internal").ap()

    s_wt = p.sem("wt")
    s_ax = p.sem("ax")
    s_v = p.sem("v")
    s_s = p.sem("s")
    s_t = p.sem("t")
    s_g = p.sem("g")

    WMAP = {"sync": s_wt, "gpsimd2": s_ax, "vector": s_v, "scalar": s_s,
            "tensor": s_t, "gpsimd": s_g}

    def aw(*engs):
        return [(WMAP[e], None) for e in engs]

    def sb(name, shape, dtype):
        return stack.enter_context(nc.sbuf_tensor("sb_" + name, list(shape), dtype))

    def pt(name, shape, dtype=F32):
        return stack.enter_context(nc.psum_tensor(name, list(shape), dtype))

    def arena():
        st = ExitStack()

        def sb2(name, shape, dtype):
            return st.enter_context(nc.sbuf_tensor("sb_" + name, list(shape), dtype))
        return st, sb2

    def dma(eng, out, in_, wait=()):
        sem = s_wt if eng == "sync" else s_ax
        p.step(eng, lambda e, o=out, i=in_: e.dma_start(out=o, in_=i),
               wait=wait, inc=[(sem, 16)])

    c_id = sb("c_id", (128, 128), BF16)
    c_tm = sb("c_tm", (128, 128), F32)
    c_ones = sb("c_ones", (128, 512), F32)
    c_onesb = sb("c_onesb", (128, 1), BF16)
    c_misc = sb("c_misc", (128, 4), F32)
    c_idf = sb("c_idf", (128, 128), F32)
    c_iota8 = sb("c_iota8", (128, 16), F32)
    w_convw = sb("w_convw", (128, 80, 4), F32)
    w_convb = sb("w_convb", (128, 80), F32)
    w_dtb = sb("w_dtb", (32, 1), F32)
    w_acol = sb("w_acol", (32, 1), F32)
    w_dskip = sb("w_dskip", (128, 16), F32)
    w_gnw = sb("w_gnw", (128, 16), F32)
    w_normw = sb("w_normw", (128, 8), F32)

    for dst, src in ((c_id[:], ident_d), (c_tm[:], tm_d), (c_ones[:], ones_d),
                     (c_onesb[:], onesb_d), (c_misc[:], misc_d), (c_idf[:], identf_d), (c_iota8[:], iota8_d),
                     (w_convw[:], convw_d), (w_convb[:], convb_d),
                     (w_dtb[:], dtb_d), (w_acol[:], alog_d), (w_dskip[:], dskip_d),
                     (w_gnw[:], gnw_d), (w_normw[:], normw_d)):
        dma("sync", dst, src)

    p.step("scalar", lambda e: e.activation(out=w_acol[:], in_=w_acol[:], func=AF.Exp),
           wait=aw("sync"), inc=[(s_s, 1)])
    p.step("vector", lambda e: e.tensor_scalar(out=w_acol[:], in0=w_acol[:], scalar1=-1.0,
                                               scalar2=None, op0=ALU.mult),
           wait=aw("scalar"), inc=[(s_v, 1)])
    p.step("gpsimd", lambda e: e.load_library(library_config.local_scatter))

    pOP = pt("pOP", (128, 1024))
    pmA = pt("pmA", (128, 512))
    ptr = pt("ptr", (128, 1024), BF16)
    psc = pt("psc", (128, 2048))

    x_t = sb("x_t", (128, 1024), F32)
    tb_t = sb("tb_t", (128, 1024), BF16)
    u_t = sb("u_t", (128, 1024), F32)
    ao_t = sb("ao_t", (128, 1024), F32)
    mcol = sb("mcol", (128, 16), F32)
    vcol = sb("vcol", (128, 16), F32)
    rsq1 = sb("rsq1", (128, 4), F32)
    rsq2 = sb("rsq2", (128, 4), F32)
    tmpc = sb("tmpc", (128, 8), F32)

    def ln_stats(src_ap, mc, vc):
        p.step("vector", lambda e: e.tensor_reduce(
            out=mc, in_=src_ap, axis=AXL.X, op=ALU.add),
            wait=aw("sync", "scalar", "vector"), inc=[(s_v, 1)])
        p.step("scalar", lambda e: e.activation(
            out=u_t[:], in_=src_ap, func=AF.Square),
            wait=aw("sync", "scalar", "vector"), inc=[(s_s, 1)])
        p.step("vector", lambda e: e.tensor_reduce(
            out=vc, in_=u_t[:], axis=AXL.X, op=ALU.add),
            wait=aw("scalar", "vector"), inc=[(s_v, 1)])
        p.step("vector", lambda e: e.tensor_scalar(
            out=mc, in0=mc, scalar1=1.0 / 1024, scalar2=None, op0=ALU.mult),
            wait=aw("vector"), inc=[(s_v, 1)])
        p.step("vector", lambda e: e.tensor_tensor(
            out=tmpc[:, 0:1], in0=mc, in1=mc, op=ALU.mult),
            wait=aw("vector"), inc=[(s_v, 1)])
        p.step("vector", lambda e: e.tensor_scalar(
            out=vc, in0=vc, scalar1=1.0 / 1024, scalar2=tmpc[:, 0:1],
            op0=ALU.mult, op1=ALU.subtract), wait=aw("vector"), inc=[(s_v, 1)])

    def rsq_from_var(vc, dst, scale_in=1.0, bias_col=0):
        p.step("scalar", lambda e: e.activation(
            out=dst, in_=vc, func=AF.Ln, bias=c_misc[:, bias_col:bias_col + 1],
            scale=scale_in), wait=aw("vector", "scalar"), inc=[(s_s, 1)])
        p.step("scalar", lambda e: e.activation(
            out=dst, in_=dst, func=AF.Exp, scale=-0.5),
            wait=aw("scalar"), inc=[(s_s, 1)])

    if DO_MAMBA:
        m_st, sb = arena()
        ipslab = [sb("ipslab0", (128, 8, 128), BF16), sb("ipslab1", (128, 8, 128), BF16)]
        ru_fm = sb("ru_fm", (128, 8, 512), BF16)
        xp0 = sb("xp0", (128, 516), BF16)
        xp = [xp0, xp0]
        cacc = sb("cacc", (128, 512), BF16)
        tails = sb("tails", (128, 80, 4), BF16)
        z_fm = sb("z_fm", (128, 16, 512), BF16)
        X_fm = sb("X_fm", (128, 16, 512), BF16)
        B_fm = sb("B_fm", (128, 32, 512), BF16)
        C_fm = sb("C_fm", (128, 32, 512), BF16)
        dt_s = sb("dt_s", (32, 512), F32)
        sps = sb("sps", (32, 512), F32)
        ln_s = sb("ln_s", (32, 512), F32)
        am_s = sb("am_s", (32, 512), F32)
        ct_fm = sb("ct_fm", (32, 128), F32)
        wf_fm = sb("wf_fm", (32, 128), F32)
        ct_tok = sb("ct_tok", (128, 32), F32)
        wf_tok = sb("wf_tok", (128, 32), F32)
        cumrow = sb("cumrow", (128, 2048), F32)
        decay = sb("decay", (128, 2048), BF16)
        expcr = sb("expcr", (128, 2048), BF16)
        scoresT = sb("scoresT", (128, 16, 128), BF16)
        B_tok = sb("B_tok", (128, 32, 128), BF16)
        X_tok = sb("X_tok", (128, 16, 128), BF16)
        Ctil = sb("Ctil", (128, 16, 128), BF16)
        Hbf = sb("Hbf", (128, 16, 64), BF16)
        carry = sb("carry", (128, 32, 64), F32)
        ydsk = sb("ydsk", (128, 8, 128), BF16)
        y_g = sb("y_g", (128, 8, 128), F32)
        ysq = ydsk
        y2_fm = sb("y2_fm", (128, 16, 512), BF16)
        rstd_c = sb("rstd_c", (128, 4), F32)

        p.step("vector", lambda e: e.memset(carry[:].rearrange("p a b -> p (a b)"), 0.0),
               inc=[(s_v, 1)])
        p.step("vector", lambda e: e.memset(tails[:].rearrange("p a b -> p (a b)"), 0.0),
               inc=[(s_v, 1)])

        for g in range(N_GROUPS):
            t0 = g * TG
            for tc in range(NCHUNK):
                row = t0 + tc * CH
                dma("sync", x_t[:], x_d[row:row + 128, :],
                    wait=aw("vector", "scalar", "tensor"))
                ln_stats(x_t[:], mcol[:, tc:tc + 1], vcol[:, tc:tc + 1])
                rsq_from_var(vcol[:, tc:tc + 1], rsq1[:, tc:tc + 1])
                rsq_from_var(vcol[:, tc:tc + 1], rsq2[:, tc:tc + 1],
                             scale_in=1.0 + EPS, bias_col=1)
                p.step("vector", lambda e, tc=tc: e.scalar_tensor_tensor(
                    out=tmpc[:, 1:2], in0=mcol[:, tc:tc + 1], scalar=-1.0,
                    in1=rsq2[:, tc:tc + 1], op0=ALU.mult, op1=ALU.mult),
                    wait=aw("scalar", "vector"), inc=[(s_v, 1)])
                p.step("scalar", lambda e, tc=tc: e.activation(
                    out=tb_t[:], in_=x_t[:], func=AF.Identity, bias=tmpc[:, 1:2],
                    scale=rsq2[:, tc:tc + 1]), wait=aw("vector", "tensor"), inc=[(s_s, 1)])
                if g == 0 and tc == 0 and "ln1" in DEBUG:
                    for nm, t in (("ln_x", x_t[:]), ("ln_m", mcol[:]),
                                  ("ln_v", vcol[:]), ("ln_r1", rsq1[:]),
                                  ("ln_r2", rsq2[:])):
                        d = dout(nm, tuple(t.shape), F32, force=True)
                        dma("sync", d, t, wait=aw("scalar", "vector"))
                    d = dout("ln_tb", (128, 1024), BF16, force=True)
                    dma("sync", d, tb_t[:], wait=aw("scalar", "vector"))
                for kt in range(8):
                    p.step("tensor", lambda e, kt=kt: e.transpose(
                        ptr[:, kt * 128:(kt + 1) * 128],
                        tb_t[:, kt * 128:(kt + 1) * 128], c_id[:]),
                        wait=aw("scalar", "vector") if kt == 0 else (),
                        inc=[(s_t, 1)] if kt == 7 else ())
                p.step("vector", lambda e, tc=tc: e.tensor_tensor(
                    out=ru_fm[:, :, tc * 128:(tc + 1) * 128],
                    in0=ptr[:].rearrange("p (k t) -> p k t", k=8),
                    in1=w_normw[:].rearrange("p k -> p k ()").to_broadcast([128, 8, 128]),
                    op=ALU.mult), wait=aw("tensor"), inc=[(s_v, 1)])

            if g == 0 and dout("ru_fm", (128, 8, 512), BF16) is not None:
                dma("sync", dbg["ru_fm"], ru_fm[:], wait=aw("vector"))

            ru_tok = s_v.n
            t_base = s_t.n
            wt_base = s_wt.n
            pv0 = s_v.n
            ld_tok = {}
            mm_tok = {}
            wb_tok = {}
            pfree_tok = {}
            conv_tok = {}
            for c in range(NCH_IP):
                buf = ipslab[c % 2]
                src = (ipj_d[c].rearrange("(k d) m -> d k m", k=8)
                       if g == 0 else d_ipbf[c])
                w = [(s_t, mm_tok.get(c - 2, t_base))]
                if g == 0:
                    w.append((s_wt, wb_tok.get(c - 2, wt_base)))
                p.step("gpsimd", lambda e, buf=buf, src=src: e.dma_start(
                    out=buf[:], in_=src), wait=w, inc=[(s_ax, 16)])
                ld_tok[c] = s_ax.n
                if g == 0:
                    dma("sync", d_ipbf[c], buf[:], wait=[(s_ax, ld_tok[c])])
                    wb_tok[c] = s_wt.n
                pmm = pOP[:, 0:512] if c % 2 == 0 else pOP[:, 512:1024]
                for kt in range(8):
                    p.step("tensor", lambda e, buf=buf, kt=kt, pmm=pmm: e.matmul(
                        pmm, buf[:, kt, :], ru_fm[:, kt, :],
                        start=(kt == 0), stop=(kt == 7)),
                        wait=([(s_ax, ld_tok[c]), (s_s, pfree_tok.get(c - 2, 0)),
                               (s_v, max(ru_tok, pv0 if c < 2 else 0))] if kt == 0 else ()),
                        inc=[(s_t, 1)] if kt == 7 else ())
                mm_tok[c] = s_t.n
                if c < 16:
                    p.step("scalar", lambda e, c=c, pmm=pmm: e.activation(
                        out=z_fm[:, c, :], in_=pmm, func=AF.Silu),
                        wait=[(s_t, mm_tok[c])], inc=[(s_s, 1)])
                    pfree_tok[c] = s_s.n
                elif c < 96:
                    cc = c - 16
                    xbuf = xp[c % 2]
                    p.step("vector", lambda e, cc=cc, xbuf=xbuf: e.tensor_copy(
                        out=xbuf[:, 0:3], in_=tails[:, cc, 0:3]), inc=[(s_v, 1)])
                    p.step("scalar", lambda e, xbuf=xbuf, pmm=pmm: e.activation(
                        out=xbuf[:, 3:515], in_=pmm, func=AF.Copy),
                        wait=[(s_t, mm_tok[c]), (s_v, conv_tok.get(c - 2, 0))],
                        inc=[(s_s, 1)])
                    pfree_tok[c] = s_s.n
                    p.step("vector", lambda e, cc=cc, xbuf=xbuf: e.tensor_copy(
                        out=tails[:, cc, 0:3], in_=xbuf[:, 512:515]),
                        wait=[(s_s, pfree_tok[c])], inc=[(s_v, 1)])
                    p.step("vector", lambda e, cc=cc, xbuf=xbuf: e.tensor_scalar(
                        out=cacc[:], in0=xbuf[:, 0:512], scalar1=w_convw[:, cc, 0:1],
                        scalar2=None, op0=ALU.mult),
                        wait=[(s_s, pfree_tok[c])], inc=[(s_v, 1)])
                    for k in range(1, 4):
                        p.step("vector", lambda e, cc=cc, k=k, xbuf=xbuf: e.scalar_tensor_tensor(
                            out=cacc[:], in0=xbuf[:, k:k + 512],
                            scalar=w_convw[:, cc, k:k + 1],
                            in1=cacc[:], op0=ALU.mult, op1=ALU.add), inc=[(s_v, 1)])
                    conv_tok[c] = s_v.n
                    if cc < 16:
                        dst = X_fm[:, cc, :]
                    elif cc < 48:
                        dst = B_fm[:, cc - 16, :]
                    else:
                        dst = C_fm[:, cc - 48, :]
                    p.step("scalar", lambda e, cc=cc, dst=dst: e.activation(
                        out=dst, in_=cacc[:], func=AF.Silu, bias=w_convb[:, cc:cc + 1]),
                        wait=[(s_v, conv_tok[c])], inc=[(s_s, 1)])
                else:
                    p.step("scalar", lambda e, pmm=pmm: e.activation(
                        out=dts1[:, :] if False else sps[:, :], in_=pmm[0:32, :],
                        func=AF.Abs, bias=w_dtb[:]),
                        wait=[(s_t, mm_tok[c])], inc=[(s_s, 1)])
                    p.step("scalar", lambda e, pmm=pmm: e.activation(
                        out=dt_s[:, :], in_=pmm[0:32, :], func=AF.Relu, bias=w_dtb[:]),
                        wait=aw("scalar"), inc=[(s_s, 1)])
                    pfree_tok[c] = s_s.n
                    p.step("scalar", lambda e: e.activation(
                        out=sps[:, :], in_=sps[:, :], func=AF.Exp, scale=-1.0),
                        wait=aw("scalar"), inc=[(s_s, 1)])
                    p.step("scalar", lambda e: e.activation(
                        out=sps[:, :], in_=sps[:, :], func=AF.Ln, bias=1.0),
                        wait=aw("scalar"), inc=[(s_s, 1)])
                    p.step("vector", lambda e: e.tensor_tensor(
                        out=dt_s[:, :], in0=dt_s[:, :], in1=sps[:, :], op=ALU.add),
                        wait=aw("scalar"), inc=[(s_v, 1)])
                    p.step("scalar", lambda e: e.activation(
                        out=ln_s[:, :], in_=dt_s[:, :], func=AF.Ln),
                        wait=aw("vector"), inc=[(s_s, 1)])
                    p.step("vector", lambda e: e.tensor_scalar(
                        out=am_s[:, :], in0=dt_s[:, :], scalar1=w_acol[:],
                        scalar2=None, op0=ALU.mult), wait=aw("scalar"), inc=[(s_v, 1)])
                    for c2 in range(NCHUNK):
                        p.step("vector", lambda e, c2=c2: e.tensor_tensor_scan(
                            out=sps[:, c2 * 128:(c2 + 1) * 128],
                            data0=c_ones[0:32, 0:128],
                            data1=am_s[:, c2 * 128:(c2 + 1) * 128],
                            initial=0.0, op0=ALU.mult, op1=ALU.add),
                            wait=aw("vector"), inc=[(s_v, 1)])
                    dma("sync", d_cum[:, :], sps[:, :], wait=aw("vector"))

            if g == 0:
                for nm, t, sh in (("z_fm", z_fm, (128, 16, 512)),
                                  ("X_fm", X_fm, (128, 16, 512)),
                                  ("B_fm", B_fm, (128, 32, 512)),
                                  ("C_fm", C_fm, (128, 32, 512))):
                    if dout(nm, sh, BF16) is not None:
                        dma("sync", dbg[nm], t[:], wait=aw("scalar", "vector"))
                for nm, t in (("dt_fm", dt_s[:, :]), ("cum_fm", sps[:, :]),
                              ("lndt", ln_s[:, :])):
                    if dout(nm, (32, 512)) is not None:
                        dma("sync", dbg[nm], t, wait=aw("vector", "scalar", "sync"))

            for cc in range(NCHUNK):
                cs = slice(cc * 128, (cc + 1) * 128)
                p.step("vector", lambda e, cs=cs: e.tensor_tensor(
                    out=ct_fm[:, :], in0=ln_s[:, cs], in1=sps[:, cs],
                    op=ALU.subtract), wait=aw("scalar", "sync"), inc=[(s_v, 1)])
                p.step("vector", lambda e, cs=cs, cc=cc: e.tensor_scalar(
                    out=wf_fm[:, :], in0=sps[:, cs], scalar1=-1.0,
                    scalar2=sps[:, cc * 128 + 127:cc * 128 + 128],
                    op0=ALU.mult, op1=ALU.add), wait=aw("vector"), inc=[(s_v, 1)])
                p.step("vector", lambda e, cs=cs: e.tensor_tensor(
                    out=wf_fm[:, :], in0=wf_fm[:, :], in1=ln_s[:, cs], op=ALU.add),
                    wait=aw("vector", "scalar"), inc=[(s_v, 1)])
                p.step("scalar", lambda e: e.activation(
                    out=wf_fm[:, :], in_=wf_fm[:, :], func=AF.Exp),
                    wait=aw("vector"), inc=[(s_s, 1)])
                p.step("tensor", lambda e: e.transpose(
                    pmA[:, 0:32], ct_fm[:, :], c_idf[0:32, 0:32]),
                    wait=aw("vector", "scalar"), inc=[(s_t, 1)])
                p.step("tensor", lambda e: e.transpose(
                    pmA[:, 32:64], wf_fm[:, :], c_idf[0:32, 0:32]), inc=[(s_t, 1)])
                p.step("vector", lambda e: e.tensor_copy(
                    out=ct_tok[:], in_=pmA[:, 0:32]), wait=aw("tensor"), inc=[(s_v, 1)])
                p.step("vector", lambda e: e.tensor_copy(
                    out=wf_tok[:], in_=pmA[:, 32:64]), wait=aw("vector"), inc=[(s_v, 1)])

                for blk in range(6):
                    def _tok_tr(e, blk=blk, cs=cs):
                        for j in range(8):
                            if blk < 2:
                                srcp = X_fm[:, blk * 8 + j, cs]
                            else:
                                srcp = B_fm[:, (blk - 2) * 8 + j, cs]
                            inst = e.transpose(ptr[:, j * 128:(j + 1) * 128],
                                               srcp, c_id[:])
                        return inst
                    p.step("tensor", _tok_tr, wait=aw("vector", "scalar"),
                           inc=[(s_t, 1)])
                    if blk < 2:
                        p.step("vector", lambda e, blk=blk: e.tensor_copy(
                            out=X_tok[:, blk * 8:(blk + 1) * 8, :].rearrange("p a b -> p (a b)"),
                            in_=ptr[:]), wait=aw("tensor"), inc=[(s_v, 1)])
                    else:
                        p.step("vector", lambda e, blk=blk: e.tensor_tensor(
                            out=B_tok[:, (blk - 2) * 8:(blk - 1) * 8, :],
                            in0=ptr[:].rearrange("p (a b) -> p a b", a=8),
                            in1=wf_tok[:, (blk - 2) * 8:(blk - 1) * 8].rearrange(
                                "p a -> p a ()").to_broadcast([128, 8, 128]),
                            op=ALU.mult), wait=aw("tensor"), inc=[(s_v, 1)])

                for hf in range(2):
                    h0 = hf * 16
                    cum_src = d_cum[h0:h0 + 16, cs].rearrange("h t -> () h t").to_broadcast(
                        [128, 16, 128])
                    dma("sync", cumrow[:].rearrange("p (h t) -> p h t", h=16), cum_src,
                        wait=aw("vector", "scalar", "tensor"))
                    p.step("scalar", lambda e: e.activation(
                        out=expcr[:], in_=cumrow[:], func=AF.Exp),
                        wait=aw("sync", "vector"), inc=[(s_s, 1)])
                    p.step("vector", lambda e, h0=h0: e.tensor_tensor(
                        out=cumrow[:].rearrange("p (a b) -> p a b", a=16),
                        in0=cumrow[:].rearrange("p (a b) -> p a b", a=16),
                        in1=ct_tok[:, h0:h0 + 16].rearrange("p a -> p a ()").to_broadcast(
                            [128, 16, 128]),
                        op=ALU.add), wait=aw("scalar"), inc=[(s_v, 1)])
                    p.step("vector", lambda e: e.tensor_tensor(
                        out=cumrow[:].rearrange("p (a b) -> p a b", a=16),
                        in0=cumrow[:].rearrange("p (a b) -> p a b", a=16),
                        in1=c_tm[:].rearrange("p b -> p () b").to_broadcast(
                            [128, 16, 128]),
                        op=ALU.add), inc=[(s_v, 1)])
                    p.step("scalar", lambda e: e.activation(
                        out=decay[:], in_=cumrow[:], func=AF.Exp),
                        wait=aw("vector"), inc=[(s_s, 1)])
                    p.step("vector", lambda e, h0=h0, cs=cs: e.tensor_tensor(
                        out=Ctil[:],
                        in0=C_fm[:, h0:h0 + 16, cs],
                        in1=expcr[:].rearrange("p (a b) -> p a b", a=16),
                        op=ALU.mult), wait=aw("scalar"), inc=[(s_v, 1)])
                    def _sc_mm(e, h0=h0, cs=cs):
                        for j in range(16):
                            inst = e.matmul(psc[:, j * 128:(j + 1) * 128],
                                            B_fm[:, h0 + j, cs], C_fm[:, h0 + j, cs],
                                            start=True, stop=True)
                        return inst
                    p.step("tensor", _sc_mm, wait=aw("vector", "scalar"),
                           inc=[(s_t, 1)])
                    p.step("vector", lambda e: e.tensor_tensor(
                        out=scoresT[:].rearrange("p a b -> p (a b)"),
                        in0=psc[:], in1=decay[:], op=ALU.mult),
                        wait=aw("tensor", "scalar"), inc=[(s_v, 1)])
                    p.step("vector", lambda e, h0=h0: e.tensor_copy(
                        out=Hbf[:].rearrange("p a b -> p (a b)"),
                        in_=carry[:, h0:h0 + 16, :].rearrange("p a b -> p (a b)")),
                        wait=aw("tensor"), inc=[(s_v, 1)])
                    def _y_mm(e, h0=h0, cs=cs):
                        for j in range(16):
                            h = h0 + j
                            xs = X_tok[:, h // 2, (h % 2) * 64:(h % 2) * 64 + 64]
                            prow = slice((j % 2) * 64, (j % 2) * 64 + 64)
                            pcol = slice((j // 2) * 128, (j // 2) * 128 + 128)
                            e.matmul(psc[prow, pcol], xs, scoresT[:, j, :],
                                     start=True, stop=False)
                            inst = e.matmul(psc[prow, pcol], Hbf[:, j, :],
                                            Ctil[:, j, :], start=False, stop=True)
                        return inst
                    p.step("tensor", _y_mm, wait=aw("vector"), inc=[(s_t, 1)])
                    def _s_mm(e, h0=h0):
                        for j in range(16):
                            h = h0 + j
                            xs = X_tok[:, h // 2, (h % 2) * 64:(h % 2) * 64 + 64]
                            inst = e.matmul(psc[:, 1024 + j * 64:1024 + (j + 1) * 64],
                                            B_tok[:, h, :], xs, start=True, stop=True)
                        return inst
                    p.step("tensor", _s_mm, inc=[(s_t, 1)])
                    p.step("vector", lambda e, h0=h0, cs=cs: e.tensor_tensor(
                        out=ydsk[:],
                        in0=X_fm[:, h0 // 2:h0 // 2 + 8, cs],
                        in1=w_dskip[:, h0 // 2:h0 // 2 + 8].rearrange(
                            "p a -> p a ()").to_broadcast([128, 8, 128]),
                        op=ALU.mult), wait=aw("tensor"), inc=[(s_v, 1)])
                    p.step("vector", lambda e: e.tensor_tensor(
                        out=y_g[:],
                        in0=psc[:, 0:1024].rearrange("p (a b) -> p a b", a=8),
                        in1=ydsk[:], op=ALU.add),
                        inc=[(s_v, 1)])
                    p.step("vector", lambda e, h0=h0, cs=cs: e.tensor_tensor(
                        out=y_g[:], in0=y_g[:],
                        in1=z_fm[:, h0 // 2:h0 // 2 + 8, cs],
                        op=ALU.mult), inc=[(s_v, 1)])
                    p.step("scalar", lambda e: e.activation(
                        out=ysq[:].rearrange("p a b -> p (a b)"),
                        in_=y_g[:].rearrange("p a b -> p (a b)"), func=AF.Square),
                        wait=aw("vector"), inc=[(s_s, 1)])
                    def _sq_mm(e, hf=hf, cc=cc):
                        for kch in range(8):
                            inst = e.matmul(pmA[:, 64 + cc:64 + cc + 1],
                                            ysq[:, kch, :], c_onesb[:],
                                            start=(hf == 0 and kch == 0),
                                            stop=(hf == 1 and kch == 7))
                        return inst
                    p.step("tensor", _sq_mm, wait=aw("scalar"), inc=[(s_t, 1)])
                    p.step("vector", lambda e, h0=h0, cs=cs: e.tensor_tensor(
                        out=y2_fm[:, h0 // 2:h0 // 2 + 8, cs],
                        in0=y_g[:],
                        in1=w_gnw[:, h0 // 2:h0 // 2 + 8].rearrange(
                            "p a -> p a ()").to_broadcast([128, 8, 128]),
                        op=ALU.mult), wait=aw("scalar"), inc=[(s_v, 1)])
                    p.step("vector", lambda e, h0=h0: e.tensor_tensor(
                        out=carry[:, h0:h0 + 16, :],
                        in0=carry[:, h0:h0 + 16, :],
                        in1=expcr[:].rearrange("p (a b) -> p a b", b=128)[:, :, 127:128].to_broadcast(
                            [128, 16, 64]),
                        op=ALU.mult), wait=aw("tensor"), inc=[(s_v, 1)])
                    p.step("vector", lambda e, h0=h0: e.tensor_tensor(
                        out=carry[:, h0:h0 + 16, :],
                        in0=carry[:, h0:h0 + 16, :],
                        in1=psc[:, 1024:2048].rearrange("p (a b) -> p a b", a=16),
                        op=ALU.add), inc=[(s_v, 1)])

                if g == 0 and cc == (1 if "chunk1" in DEBUG else 0):
                    for nm, t, sh, dt_ in (
                            ("scoresT", scoresT, (128, 16, 128), BF16),
                            ("decay", decay, (128, 2048), BF16),
                            ("X_tok", X_tok, (128, 16, 128), BF16),
                            ("B_tok", B_tok, (128, 32, 128), BF16),
                            ("Ctil", Ctil, (128, 16, 128), BF16),
                            ("Hbf", Hbf, (128, 16, 64), BF16)):
                        if dout(nm, sh, dt_) is not None:
                            dma("sync", dbg[nm], t[:], wait=aw("vector", "scalar"))

            if g == 0 and dout("y2c0", (128, 16, 512), BF16) is not None:
                dma("sync", dbg["y2c0"], y2_fm[:], wait=aw("vector", "scalar"))
            if g == 0 and "carry" in DEBUG:
                d = dout("carry", (128, 32, 64), F32, force=True)
                dma("sync", d, carry[:], wait=aw("vector", "scalar", "tensor"))

            for tc in range(NCHUNK):
                p.step("scalar", lambda e, tc=tc: e.activation(
                    out=rstd_c[:, tc:tc + 1], in_=pmA[:, 64 + tc:64 + tc + 1],
                    func=AF.Ln, bias=c_misc[:, 0:1], scale=1.0 / 2048),
                    wait=aw("tensor", "vector"), inc=[(s_s, 1)])
                p.step("scalar", lambda e, tc=tc: e.activation(
                    out=rstd_c[:, tc:tc + 1], in_=rstd_c[:, tc:tc + 1],
                    func=AF.Exp, scale=-0.5), wait=aw("scalar"), inc=[(s_s, 1)])

            y2_tok = s_v.n
            t0op = s_t.n
            opld = {}
            opmm = {}
            pread_tok = {0: 0, 1: 0}
            first_op = True
            for half in range(2):
                for kt in range(16):
                    buf = ipslab[kt % 2]
                    i_lin = half * 16 + kt
                    p.step("gpsimd", lambda e, buf=buf, kt=kt: e.dma_start(
                        out=buf[:].rearrange("p a b -> p (a b)"), in_=opj_d[kt]),
                        wait=[(s_t, opmm.get(i_lin - 2, t0op))], inc=[(s_ax, 16)])
                    opld[i_lin] = s_ax.n

                    for tci in range(2):
                        tc = half * 2 + tci
                        pmm = pOP if tci == 0 else psc
                        def _op_mm(e, buf=buf, kt=kt, tc=tc, pmm=pmm):
                            bv = buf[:].rearrange("p a b -> p (a b)")
                            for nn in range(2):
                                inst = e.matmul(
                                    pmm[:, nn * 512:(nn + 1) * 512],
                                    y2_fm[:, kt, tc * 128:(tc + 1) * 128],
                                    bv[:, nn * 512:(nn + 1) * 512],
                                    start=(kt == 0), stop=(kt == 15))
                            return inst
                        w = [(s_ax, opld[i_lin])] if tci == 0 else []
                        if kt == 0:
                            w = w + [(s_v, pread_tok[tci])]
                        if first_op:
                            w = w + aw("vector", "scalar")
                            if tci == 1:
                                first_op = False
                        p.step("tensor", _op_mm, wait=w,
                               inc=[(s_t, 1)] if tci == 1 else ())
                    opmm[i_lin] = s_t.n
                for tci in range(2):
                    tc = half * 2 + tci
                    row = t0 + tc * CH
                    pmm = pOP if tci == 0 else psc
                    dma("sync", x_t[:], x_d[row:row + 128, :],
                        wait=aw("vector", "scalar", "tensor"))
                    p.step("vector", lambda e, tc=tc: e.scalar_tensor_tensor(
                        out=tmpc[:, 2:3], in0=mcol[:, tc:tc + 1], scalar=-1.0,
                        in1=rsq1[:, tc:tc + 1], op0=ALU.mult, op1=ALU.mult),
                        wait=aw("sync", "vector"), inc=[(s_v, 1)])
                    p.step("scalar", lambda e, tc=tc: e.activation(
                        out=u_t[:], in_=x_t[:], func=AF.Identity, bias=tmpc[:, 2:3],
                        scale=rsq1[:, tc:tc + 1]), wait=aw("vector"), inc=[(s_s, 1)])
                    p.step("vector", lambda e, tc=tc, pmm=pmm: e.scalar_tensor_tensor(
                        out=ao_t[:], in0=pmm[:, 0:1024], scalar=rstd_c[:, tc:tc + 1],
                        in1=u_t[:], op0=ALU.mult, op1=ALU.add),
                        wait=aw("scalar", "tensor"), inc=[(s_v, 1)])
                    pread_tok[tci] = s_v.n
                    ln_stats(ao_t[:], tmpc[:, 3:4], tmpc[:, 4:5])
                    rsq_from_var(tmpc[:, 4:5], tmpc[:, 5:6])
                    p.step("vector", lambda e: e.scalar_tensor_tensor(
                        out=tmpc[:, 6:7], in0=tmpc[:, 3:4], scalar=-1.0,
                        in1=tmpc[:, 5:6], op0=ALU.mult, op1=ALU.mult),
                        wait=aw("scalar", "vector"), inc=[(s_v, 1)])
                    p.step("scalar", lambda e: e.activation(
                        out=x_t[:], in_=x_t[:], func=AF.Identity, bias=tmpc[:, 6:7]),
                        wait=aw("vector"), inc=[(s_s, 1)])
                    p.step("vector", lambda e: e.scalar_tensor_tensor(
                        out=x_t[:], in0=ao_t[:], scalar=tmpc[:, 5:6],
                        in1=x_t[:], op0=ALU.mult, op1=ALU.add),
                        wait=aw("scalar"), inc=[(s_v, 1)])
                    dma("sync", d_x1[row:row + 128, :], x_t[:], wait=aw("vector"))

        m_st.close()

    if "x1" in DEBUG:
        d = dout("x1", (L, D))
        dma("sync", d, d_x1, wait=aw("vector", "scalar", "tensor", "sync"))

    hid_t = stack.enter_context(nc.sbuf_tensor("sb_hid_t", [128, 8, 128], BF16))
    fa_st, sb = arena()
    wq_sb = sb("wq_sb", (128, 8, 512), BF16)
    keys_sb = sb("keys_sb", (128, 512), BF16)
    q_bf = sb("q_bf", (128, 4, 128), BF16)
    sim_sb = sb("sim_sb", (128, 512), F32)
    mx8 = sb("mx8", (128, 2, 4, 8), F32)
    ixu = sb("ixu", (128, 2, 4, 8), U32)
    ixf = sb("ixf", (128, 2, 4, 8), F32)
    cmb = sb("cmb", (128, 4, 64), F32)
    v8h = sb("v8h", (128, 4, 8), F32)
    c8u = sb("c8u", (128, 4, 8), U32)
    c8f = sb("c8f", (128, 4, 8), F32)
    j8 = sb("j8", (128, 4, 8), F32)
    i8 = sb("i8", (128, 4, 8), F32)
    oh8 = sb("oh8", (128, 4, 8, 8), F32)
    ixg = sb("ixg", (128, 4, 8), F32)
    iyg = sb("iyg", (128, 4, 8), F32)
    ec8 = sb("ec8", (128, 4, 8), F32)
    mq = sb("mq", (128, 4, 8), F32)
    eq = sb("eq", (128, 4, 8), F32)
    p8 = sb("p8", (128, 4, 8), F32)
    den = sb("den", (128, 4), F32)
    pc_bf = sb("pc_bf", (128, 32), BF16)
    ix_i16 = sb("ix_i16", (128, 96), I16)

    if DO_FFN_A:
        dma("gpsimd", wq_sb[:], wq_d, wait=aw("tensor", "vector", "scalar"))
        dma("gpsimd", keys_sb[:], keys_d)
        for ti in range(N_FFN_TILES):
            row = ti * 128
            dma("sync", x_t[:], d_x1[row:row + 128, :],
                wait=aw("vector", "scalar", "tensor"))
            ln_stats(x_t[:], mcol[:, 0:1], vcol[:, 0:1])
            rsq_from_var(vcol[:, 0:1], rsq1[:, 0:1])
            p.step("vector", lambda e: e.scalar_tensor_tensor(
                out=tmpc[:, 1:2], in0=mcol[:, 0:1], scalar=-1.0,
                in1=rsq1[:, 0:1], op0=ALU.mult, op1=ALU.mult),
                wait=aw("scalar"), inc=[(s_v, 1)])
            p.step("scalar", lambda e: e.activation(
                out=tb_t[:], in_=x_t[:], func=AF.Identity, bias=tmpc[:, 1:2],
                scale=rsq1[:, 0:1]), wait=aw("vector", "tensor"), inc=[(s_s, 1)])
            for kt in range(8):
                p.step("tensor", lambda e, kt=kt: e.transpose(
                    ptr[:, kt * 128:(kt + 1) * 128],
                    tb_t[:, kt * 128:(kt + 1) * 128], c_id[:]),
                    wait=aw("scalar", "vector") if kt == 0 else (),
                    inc=[(s_t, 1)] if kt == 7 else ())
            p.step("vector", lambda e: e.tensor_copy(
                out=hid_t[:].rearrange("p a b -> p (a b)"), in_=ptr[:]),
                wait=aw("tensor"), inc=[(s_v, 1)])
            dma("sync", d_hid[ti].rearrange("k d t -> d k t"), hid_t[:], wait=aw("vector"))
            if FA_CUT <= 1:
                continue
            def _q_mm(e):
                for qc in range(4):
                    for kt in range(8):
                        inst = e.matmul(pmA[:, qc * 128:(qc + 1) * 128],
                                        wq_sb[:, kt, qc * 128:(qc + 1) * 128],
                                        hid_t[:, kt, :],
                                        start=(kt == 0), stop=(kt == 7))
                return inst
            p.step("tensor", _q_mm, wait=aw("vector", "gpsimd2"), inc=[(s_t, 1)])
            p.step("vector", lambda e: e.tensor_copy(
                out=q_bf[:].rearrange("p a b -> p (a b)"), in_=pmA[:]),
                wait=aw("tensor"), inc=[(s_v, 1)])
            if FA_CUT <= 2:
                continue
            def _sim_mm(e):
                for blk in range(8):
                    p_, h = blk // 4, blk % 4
                    qc = (p_ * 256 + h * 64) // 128
                    inst = e.matmul(pmA[:, blk * 64:(blk + 1) * 64],
                                    q_bf[:, qc, :],
                                    keys_sb[:, blk * 64:(blk + 1) * 64],
                                    start=True, stop=True)
                return inst
            p.step("tensor", _sim_mm, wait=aw("vector"), inc=[(s_t, 1)])
            p.step("vector", lambda e: e.tensor_copy(out=sim_sb[:], in_=pmA[:]),
                   wait=aw("tensor"), inc=[(s_v, 1)])
            if FA_CUT <= 3:
                continue
            def _max8(e):
                for blk in range(8):
                    p_, h = blk // 4, blk % 4
                    inst = e.max(out=mx8[:, p_, h, :],
                                 in_=sim_sb[:, blk * 64:(blk + 1) * 64])
                return inst
            p.step("vector", _max8, inc=[(s_v, 1)])
            def _mi8(e):
                for blk in range(8):
                    p_, h = blk // 4, blk % 4
                    inst = e.max_index(out=ixu[:, p_, h, :],
                                       in_max=mx8[:, p_, h, :],
                                       in_values=sim_sb[:, blk * 64:(blk + 1) * 64])
                return inst
            p.step("vector", _mi8, wait=aw("vector"), inc=[(s_v, 1)])
            p.step("vector", lambda e: e.tensor_tensor(
                out=ixf[:].rearrange("p a b c -> p (a b c)"),
                in0=ixu[:].rearrange("p a b c -> p (a b c)"),
                in1=c_ones[:, 0:64], op=ALU.mult),
                wait=aw("vector"), inc=[(s_v, 1)])
            if FA_CUT <= 4:
                continue
            p.step("vector", lambda e: e.tensor_tensor(
                out=cmb[:].rearrange("p h (i j) -> p h i j", j=8),
                in0=mx8[:, 0, :, :].rearrange("p h i -> p h i ()").to_broadcast([128, 4, 8, 8]),
                in1=mx8[:, 1, :, :].rearrange("p h j -> p h () j").to_broadcast([128, 4, 8, 8]),
                op=ALU.add), wait=aw("vector"), inc=[(s_v, 1)])
            def _v8(e):
                for h in range(4):
                    inst = e.max(out=v8h[:, h, :], in_=cmb[:, h, :])
                return inst
            p.step("vector", _v8, wait=aw("vector"), inc=[(s_v, 1)])
            def _c8(e):
                for h in range(4):
                    inst = e.max_index(out=c8u[:, h, :], in_max=v8h[:, h, :],
                                       in_values=cmb[:, h, :])
                return inst
            p.step("vector", _c8, wait=aw("vector"), inc=[(s_v, 1)])
            p.step("vector", lambda e: e.tensor_tensor(
                out=c8f[:].rearrange("p a b -> p (a b)"),
                in0=c8u[:].rearrange("p a b -> p (a b)"),
                in1=c_ones[:, 0:32], op=ALU.mult),
                wait=aw("vector"), inc=[(s_v, 1)])
            for h in range(4):
                p.step("vector", lambda e, h=h: e.tensor_tensor(
                    out=oh8[:, h, :, 0:7],
                    in0=c8f[:, h, :].rearrange("p k -> p k ()").to_broadcast([128, 8, 7]),
                    in1=c_iota8[:, 8:15].rearrange("p m -> p () m").to_broadcast([128, 8, 7]),
                    op=ALU.is_ge), wait=aw("vector"), inc=[(s_v, 1)])
                p.step("vector", lambda e, h=h: e.tensor_reduce(
                    out=i8[:, h, :], in_=oh8[:, h, :, 0:7], axis=AXL.X, op=ALU.add),
                    wait=aw("vector"), inc=[(s_v, 1)])
            p.step("vector", lambda e: e.scalar_tensor_tensor(
                out=j8[:].rearrange("p a b -> p (a b)"),
                in0=i8[:].rearrange("p a b -> p (a b)"), scalar=-8.0,
                in1=c8f[:].rearrange("p a b -> p (a b)"),
                op0=ALU.mult, op1=ALU.add), wait=aw("vector"), inc=[(s_v, 1)])
            if ti == 0 and "fadbg" in DEBUG:
                d = dout("j8e", (128, 4, 8), F32, force=True)
                dma("sync", d, j8[:], wait=aw("vector"))
                d = dout("i8e", (128, 4, 8), F32, force=True)
                dma("sync", d, i8[:], wait=aw("vector"))
            if FA_CUT <= 5:
                continue
            for (sel, src_p, dst) in ((i8, 0, ixg), (j8, 1, iyg)):
                for h in range(4):
                    p.step("vector", lambda e, sel=sel, h=h: e.tensor_tensor(
                        out=oh8[:, h, :, :],
                        in0=sel[:, h, :].rearrange("p k -> p k ()").to_broadcast([128, 8, 8]),
                        in1=c_iota8[:, 0:8].rearrange("p m -> p () m").to_broadcast([128, 8, 8]),
                        op=ALU.is_equal), wait=aw("vector"), inc=[(s_v, 1)])
                    p.step("vector", lambda e, src_p=src_p, h=h: e.tensor_tensor(
                        out=oh8[:, h, :, :], in0=oh8[:, h, :, :],
                        in1=ixf[:, src_p, h, :].rearrange("p m -> p () m").to_broadcast(
                            [128, 8, 8]), op=ALU.mult), wait=aw("vector"), inc=[(s_v, 1)])
                    p.step("vector", lambda e, dst=dst, h=h: e.tensor_reduce(
                        out=dst[:, h, :], in_=oh8[:, h, :, :], axis=AXL.X, op=ALU.add),
                        wait=aw("vector"), inc=[(s_v, 1)])
            p.step("vector", lambda e: e.scalar_tensor_tensor(
                out=ec8[:].rearrange("p a b -> p (a b)"),
                in0=ixg[:].rearrange("p a b -> p (a b)"), scalar=64.0,
                in1=iyg[:].rearrange("p a b -> p (a b)"),
                op0=ALU.mult, op1=ALU.add), wait=aw("vector"), inc=[(s_v, 1)])
            if ti == 0 and "fadbg" in DEBUG:
                for nm, t, sh in (("i8d", i8, (128, 4, 8)), ("j8d", j8, (128, 4, 8)),
                                  ("ixgd", ixg, (128, 4, 8)), ("iygd", iyg, (128, 4, 8)),
                                  ("ec8d", ec8, (128, 4, 8)), ("c8d", c8f, (128, 4, 8))):
                    d = dout(nm, sh, F32, force=True)
                    dma("sync", d, t[:], wait=aw("vector", "scalar"))
            if FA_CUT <= 6:
                continue
            for ri, (lo, hi) in enumerate(((0, 1366), (1366, 2732), (2732, 4096))):
                p.step("vector", lambda e, lo=lo: e.scalar_tensor_tensor(
                    out=mq[:].rearrange("p a b -> p (a b)"),
                    in0=ec8[:].rearrange("p a b -> p (a b)"), scalar=float(lo),
                    in1=c_ones[:, 0:32], op0=ALU.is_ge, op1=ALU.mult),
                    wait=aw("vector"), inc=[(s_v, 1)])
                p.step("vector", lambda e, hi=hi: e.scalar_tensor_tensor(
                    out=eq[:].rearrange("p a b -> p (a b)"),
                    in0=ec8[:].rearrange("p a b -> p (a b)"), scalar=float(hi),
                    in1=c_ones[:, 0:32], op0=ALU.is_lt, op1=ALU.mult),
                    wait=aw("vector"), inc=[(s_v, 1)])
                p.step("vector", lambda e: e.tensor_tensor(
                    out=mq[:].rearrange("p a b -> p (a b)"),
                    in0=mq[:].rearrange("p a b -> p (a b)"),
                    in1=eq[:].rearrange("p a b -> p (a b)"), op=ALU.mult),
                    inc=[(s_v, 1)])
                p.step("vector", lambda e, lo=lo: e.scalar_tensor_tensor(
                    out=eq[:].rearrange("p a b -> p (a b)"),
                    in0=ec8[:].rearrange("p a b -> p (a b)"), scalar=float(-lo),
                    in1=mq[:].rearrange("p a b -> p (a b)"), op0=ALU.add, op1=ALU.mult),
                    wait=aw("vector"), inc=[(s_v, 1)])
                p.step("vector", lambda e: e.tensor_tensor(
                    out=eq[:].rearrange("p a b -> p (a b)"),
                    in0=eq[:].rearrange("p a b -> p (a b)"),
                    in1=mq[:].rearrange("p a b -> p (a b)"), op=ALU.add),
                    wait=aw("vector"), inc=[(s_v, 1)])
                p.step("vector", lambda e, ri=ri: e.scalar_tensor_tensor(
                    out=ix_i16[:, ri * 32:(ri + 1) * 32],
                    in0=eq[:].rearrange("p a b -> p (a b)"), scalar=-1.0,
                    in1=c_ones[:, 0:32], op0=ALU.add, op1=ALU.mult),
                    wait=aw("vector"), inc=[(s_v, 1)])
            if FA_CUT <= 7:
                continue
            p.step("vector", lambda e: e.tensor_tensor(
                out=p8[:], in0=v8h[:],
                in1=v8h[:, :, 0:1].to_broadcast([128, 4, 8]), op=ALU.subtract),
                wait=aw("vector"), inc=[(s_v, 1)])
            p.step("scalar", lambda e: e.activation(
                out=p8[:].rearrange("p a b -> p (a b)"),
                in_=p8[:].rearrange("p a b -> p (a b)"), func=AF.Exp),
                wait=aw("vector"), inc=[(s_s, 1)])
            p.step("vector", lambda e: e.tensor_reduce(
                out=den[:], in_=p8[:], axis=AXL.X, op=ALU.add),
                wait=aw("scalar"), inc=[(s_v, 1)])
            p.step("vector", lambda e: e.reciprocal(out=den[:], in_=den[:]),
                   wait=aw("vector"), inc=[(s_v, 1)])
            p.step("vector", lambda e: e.tensor_tensor(
                out=pc_bf[:].rearrange("p (a b) -> p a b", a=4),
                in0=p8[:],
                in1=den[:].rearrange("p a -> p a ()").to_broadcast([128, 4, 8]),
                op=ALU.mult), wait=aw("vector"), inc=[(s_v, 1)])
            dma("sync", d_pc[ti], pc_bf[:], wait=aw("vector"))
            dma("sync", d_ix[ti], ix_i16[:], wait=aw("vector"))
            if ti == 0:
                for nm, t, sh, dt_ in (("sim", sim_sb, (128, 512), F32),
                                       ("mx8", mx8, (128, 2, 4, 8), F32),
                                       ("ixf", ixf, (128, 2, 4, 8), F32),
                                       ("pc0", pc_bf, (128, 32), BF16),
                                       ("ix0", ix_i16, (128, 96), I16)):
                    if dout(nm, sh, dt_) is not None:
                        dma("sync", dbg[nm], t[:], wait=aw("vector"))

    fa_st.close()

    if DO_FFN_B:
        fb_st, sb = arena()
        wup_sb = sb("wup_sb", (128, 8, 4096), BF16)
        hidH = sb("hidH", (128, 8, 2048), BF16)
        up_t = sb("up_t", (128, 2048), BF16)
        dma("gpsimd", wup_sb[:], wup_d, wait=aw("tensor", "vector", "scalar"))
        for half in range(2):
            hh = half * 2048
            for a_ in range(16):
                dma("sync", hidH[:, :, a_ * 128:(a_ + 1) * 128],
                    d_hid[half * 16 + a_].rearrange("k d t -> d k t"),
                    wait=aw("tensor") if a_ == 0 else ())
            for ic in range(32):
                def _up_mm(e, ic=ic):
                    for kt in range(8):
                        for gs in range(4):
                            inst = e.matmul(psc[:, gs * 512:(gs + 1) * 512],
                                            wup_sb[:, kt, ic * 128:(ic + 1) * 128],
                                            hidH[:, kt, gs * 512:(gs + 1) * 512],
                                            start=(kt == 0), stop=(kt == 7))
                    return inst
                p.step("tensor", _up_mm,
                       wait=aw("sync", "gpsimd2") if ic == 0 else aw("scalar"),
                       inc=[(s_t, 1)])
                p.step("scalar", lambda e: e.activation(
                    out=up_t[:], in_=psc[:], func=AF.Silu),
                    wait=aw("tensor", "sync"), inc=[(s_s, 1)])
                dma("sync", d_up[ic][:, hh:hh + 2048], up_t[:], wait=aw("scalar"))
        fb_st.close()
        fb2_st, sb = arena()
        wdn_sb = sb("wdn_sb", (128, 32, D), BF16)
        upt_sb = sb("upt_sb", (128, 32, 128), BF16)
        mlp_t = sb("mlp_t", (128, 1024), BF16)
        dma("gpsimd", wdn_sb[:], wdn_d, wait=aw("tensor", "scalar", "sync"))
        for ti in range(N_FFN_TILES):
            src = d_up[:, :, ti * 128:(ti + 1) * 128].rearrange("a d t -> d a t")
            dma("sync", upt_sb[:], src, wait=aw("tensor", "vector"))
            def _dn_mm(e, ti=ti):
                for ic in range(32):
                    for nn in range(2):
                        inst = e.matmul(pOP[:, nn * 512:(nn + 1) * 512],
                                        upt_sb[:, ic, :],
                                        wdn_sb[:, ic, nn * 512:(nn + 1) * 512],
                                        start=(ic == 0), stop=(ic == 31))
                return inst
            p.step("tensor", _dn_mm, wait=aw("sync", "gpsimd2", "vector"),
                   inc=[(s_t, 1)])
            p.step("vector", lambda e: e.tensor_copy(out=mlp_t[:], in_=pOP[:]),
                   wait=aw("tensor", "sync"), inc=[(s_v, 1)])
            dma("sync", d_mlp[ti * 128:(ti + 1) * 128, :], mlp_t[:],
                wait=aw("vector"))

    if DO_FFN_B:
        fb2_st.close()
    if DO_FFN_C:
        fc_st, sb = arena()
        det_sb = sb("det_sb", (128, 8, 4096), BF16)
        ue_sb = sb("ue_sb", (128, 32, D), BF16)
        S_bf = sb("S_bf", (128, 4096), BF16)
        P_h = sb("P_h", (128, 4096), BF16)
        Wsil = sb("Wsil", (128, 4096), BF16)
        Wacc = sb("Wacc", (128, 4096), BF16)
        WT = sb("WT", (128, 32, 128), BF16)
        pc_t = sb("pc_t", (128, 32), BF16)
        ix_t = sb("ix_t", (128, 96), I16)
        mlp2_t = sb("mlp2_t", (128, 1024), BF16)
        dma("gpsimd", det_sb[:], det_d, wait=aw("tensor", "vector", "scalar"))
        dma("gpsimd", ue_sb[:], ue_d)
        for ti in range(N_FFN_TILES):
            row = ti * 128
            dma("sync", hid_t[:], d_hid[ti].rearrange("k d t -> d k t"), wait=aw("tensor", "vector", "scalar"))
            dma("sync", pc_t[:], d_pc[ti], wait=aw("gpsimd"))
            dma("sync", ix_t[:], d_ix[ti], wait=aw("gpsimd"))
            dma("sync", mlp2_t[:], d_mlp[row:row + 128, :], wait=aw("vector"))
            dma("sync", x_t[:], d_x1[row:row + 128, :], wait=aw("vector", "scalar"))
            for r in range(2):
                def _s_mm(e, r=r):
                    for kt in range(8):
                        for nn in range(4):
                            inst = e.matmul(psc[:, nn * 512:(nn + 1) * 512],
                                            hid_t[:, kt, :],
                                            det_sb[:, kt, r * 2048 + nn * 512:
                                                   r * 2048 + (nn + 1) * 512],
                                            start=(kt == 0), stop=(kt == 7))
                    return inst
                p.step("tensor", _s_mm,
                       wait=aw("sync", "gpsimd2") if r == 0 else aw("vector"),
                       inc=[(s_t, 1)])
                p.step("vector", lambda e, r=r: e.tensor_copy(
                    out=S_bf[:, r * 2048:(r + 1) * 2048], in_=psc[:]),
                    wait=aw("tensor"), inc=[(s_v, 1)])
            for h in range(4):
                for ri, (lo, hi) in enumerate(((0, 1366), (1366, 2732), (2732, 4096))):
                    p.step("gpsimd", lambda e, h=h, ri=ri, lo=lo, hi=hi: e.local_scatter(
                        out_ap=P_h[:, lo:hi], data_ap=pc_t[:, h * 8:(h + 1) * 8],
                        idxs_ap=ix_t[:, ri * 32 + h * 8:ri * 32 + (h + 1) * 8],
                        channels=128, num_elems=hi - lo, num_idxs=8),
                        wait=aw("sync", "vector") if ri == 0 else (),
                        inc=[(s_g, 1)])
                p.step("vector", lambda e: e.tensor_tensor(
                    out=Wsil[:], in0=S_bf[:], in1=P_h[:], op=ALU.mult),
                    wait=aw("gpsimd", "scalar"), inc=[(s_v, 1)])
                p.step("scalar", lambda e, h=h: e.activation(
                    out=(Wacc[:] if h == 0 else Wsil[:]), in_=Wsil[:], func=AF.Silu),
                    wait=aw("vector"), inc=[(s_s, 1)])
                if h > 0:
                    p.step("vector", lambda e: e.tensor_tensor(
                        out=Wacc[:], in0=Wacc[:], in1=Wsil[:], op=ALU.add),
                        wait=aw("scalar"), inc=[(s_v, 1)])
            for bq in range(4):
                def _wt_tr(e, bq=bq):
                    for j in range(8):
                        ecn = bq * 8 + j
                        inst = e.transpose(ptr[:, j * 128:(j + 1) * 128],
                                           Wacc[:, ecn * 128:(ecn + 1) * 128], c_id[:])
                    return inst
                p.step("tensor", _wt_tr, wait=aw("vector", "scalar"), inc=[(s_t, 1)])
                p.step("vector", lambda e, bq=bq: e.tensor_copy(
                    out=WT[:, bq * 8:(bq + 1) * 8, :].rearrange("p a b -> p (a b)"),
                    in_=ptr[:]), wait=aw("tensor"), inc=[(s_v, 1)])
            def _ue_mm(e):
                for ec in range(32):
                    for nn in range(2):
                        inst = e.matmul(pOP[:, nn * 512:(nn + 1) * 512],
                                        WT[:, ec, :],
                                        ue_sb[:, ec, nn * 512:(nn + 1) * 512],
                                        start=(ec == 0), stop=(ec == 31))
                return inst
            p.step("tensor", _ue_mm, wait=aw("vector"), inc=[(s_t, 1)])
            p.step("vector", lambda e: e.tensor_tensor(
                out=ao_t[:], in0=pOP[:], in1=mlp2_t[:], op=ALU.add),
                wait=aw("tensor", "sync"), inc=[(s_v, 1)])
            ln_stats(ao_t[:], tmpc[:, 3:4], tmpc[:, 4:5])
            rsq_from_var(tmpc[:, 4:5], tmpc[:, 5:6])
            p.step("vector", lambda e: e.scalar_tensor_tensor(
                out=tmpc[:, 6:7], in0=tmpc[:, 3:4], scalar=-1.0,
                in1=tmpc[:, 5:6], op0=ALU.mult, op1=ALU.mult),
                wait=aw("scalar"), inc=[(s_v, 1)])
            p.step("scalar", lambda e: e.activation(
                out=x_t[:], in_=x_t[:], func=AF.Identity, bias=tmpc[:, 6:7]),
                wait=aw("vector"), inc=[(s_s, 1)])
            p.step("vector", lambda e: e.scalar_tensor_tensor(
                out=x_t[:], in0=ao_t[:], scalar=tmpc[:, 5:6],
                in1=x_t[:], op0=ALU.mult, op1=ALU.add),
                wait=aw("scalar"), inc=[(s_v, 1)])
            dma("sync", out_d[row:row + 128, :], x_t[:], wait=aw("vector"))
            if ti == 0:
                for nm, t, sh, dt_ in (("S_bf", S_bf, (128, 4096), BF16),
                                       ("Wacc", Wacc, (128, 4096), BF16)):
                    if dout(nm, sh, dt_) is not None:
                        dma("sync", dbg[nm], t[:], wait=aw("vector", "scalar"))

    p.step("gpsimd", lambda e: e.nop(), wait=aw("sync", "gpsimd2"))

    p.replay()
    mybir.codegen_inst_isa_subclasses(nc)
    return nc, p.n_inst


def _prep_inputs(inp):
    f32 = np.float32
    ipw = np.asarray(inp["L0_attn_in_proj_w"], f32)
    ipw = np.pad(ipw, ((0, 96), (0, 0)))
    ipj = np.ascontiguousarray(ipw.reshape(NCH_IP, 128, D).transpose(0, 2, 1))
    convw = np.ascontiguousarray(
        np.asarray(inp["L0_attn_conv_w"], f32).reshape(80, 128, 4).transpose(1, 0, 2))
    convb = np.ascontiguousarray(
        np.asarray(inp["L0_attn_conv_b"], f32).reshape(80, 128).T)
    dtb = np.asarray(inp["L0_attn_dt_bias"], f32).reshape(32, 1)
    alog = np.asarray(inp["L0_attn_A_log"], f32).reshape(32, 1)
    dskip = np.ascontiguousarray(
        np.repeat(np.asarray(inp["L0_attn_D_skip"], f32), 64).reshape(16, 128).T)
    gnw = np.ascontiguousarray(
        np.asarray(inp["L0_attn_gnorm_w"], f32).reshape(16, 128).T)
    normw = np.ascontiguousarray(
        np.asarray(inp["L0_attn_norm_w"], f32).reshape(8, 128).T)
    opj = np.ascontiguousarray(
        np.asarray(inp["L0_attn_out_proj_w"], f32).T.reshape(16, 128, D))
    wq = np.ascontiguousarray(
        np.asarray(inp["L0_ffn_W_q"], f32).T.reshape(8, 128, 512).transpose(1, 0, 2))
    keys0 = np.asarray(inp["L0_ffn_keys"], f32).transpose(3, 2, 0, 1).reshape(64, 512)
    keys = np.zeros((128, 512), f32)
    for blk in range(8):
        off = (blk // 4 * 256 + (blk % 4) * 64) % 128
        keys[off:off + 64, blk * 64:(blk + 1) * 64] = keys0[:, blk * 64:(blk + 1) * 64]
    wup = np.ascontiguousarray(
        np.asarray(inp["L0_ffn_W_up"], f32).T.reshape(8, 128, 4096).transpose(1, 0, 2))
    wdn = np.ascontiguousarray(
        np.asarray(inp["L0_ffn_W_down"], f32).T.reshape(32, 128, D).transpose(1, 0, 2))
    det = np.ascontiguousarray(
        np.asarray(inp["L0_ffn_down_embed"], f32).T.reshape(8, 128, 4096).transpose(1, 0, 2))
    ue = np.ascontiguousarray(
        np.asarray(inp["L0_ffn_up_embed"], f32).reshape(32, 128, D).transpose(1, 0, 2))

    ident = np.eye(128, dtype=ml_dtypes.bfloat16)
    tm = np.where(np.arange(128)[None, :] >= np.arange(128)[:, None],
                  0.0, NEG).astype(f32)
    ones = np.ones((128, 512), f32)
    onesb = np.ones((128, 1), ml_dtypes.bfloat16)
    misc = np.tile(np.array([EPS, 1e-10, 1.0, 0.0], f32), (128, 1))
    identf = np.eye(128, dtype=f32)
    iota8 = np.tile(np.concatenate([np.arange(8), np.arange(8, 64, 8), [0.0]]).astype(f32),
                    (128, 1))

    w = dict(w_ipj=ipj, w_convw=convw, w_convb=convb, w_dtb=dtb, w_alog=alog,
             w_dskip=dskip, w_gnw=gnw, w_normw=normw, w_opj=opj, w_wq=wq,
             w_keys=keys, w_wup=wup, w_wdn=wdn, w_det=det, w_ue=ue,
             c_ident=ident, c_tm=tm, c_ones=ones, c_onesb=onesb, c_misc=misc, c_identf=identf, c_iota8=iota8)
    return w


def kernel(**inputs):
    w = _prep_inputs(inputs)
    x = np.asarray(inputs["L0_attn_x"], np.float32)
    nc, _ = build_nc()
    in_maps = []
    for b in range(8):
        m = dict(w)
        m["x"] = np.ascontiguousarray(x[b])
        in_maps.append(m)
    res = run_bass_kernel_spmd(nc, in_maps, core_ids=list(range(8)))
    return np.stack([r["out"] for r in res.results], axis=0)
